# version 25 (final)
# speedup vs baseline: 1.2197x; 1.0040x over previous
import contextlib
import numpy as np
import ml_dtypes

import concourse.bass as bass
import concourse.mybir as mybir
from concourse.bass_utils import run_bass_kernel_spmd

F32 = mybir.dt.float32
BF16 = mybir.dt.bfloat16
FP16 = mybir.dt.float16
AF = mybir.ActivationFunctionType
ALU = mybir.AluOpType
AX = mybir.AxisListType

D = 1024
S = 4096
SH = 2048
H = 16
DK = 64
E = 8
HID0 = 4096
HID1 = 2048
EPS = 1e-5
SWAPMASK = [i ^ 1 for i in range(32)]


class Sem:
    def __init__(self, ctx, nc, name):
        self.h = ctx.enter_context(nc.semaphore(name))
        self.v = 0

    def inc(self, instr, n=16):
        instr.then_inc(self.h, n)
        self.v += n


class CastFeeder:

    def __init__(self, em, jobs, buf, depth=4):
        self.em = em
        self.buf = buf
        self.depth = depth
        self.tiles = []
        self.job_end = {}
        for name, src, dst, rows, cols in jobs:
            for i in range(rows // 128):
                for c0 in range(0, cols, 1024):
                    cc = min(1024, cols - c0)
                    self.tiles.append(
                        (src[i * 128:(i + 1) * 128, c0:c0 + cc],
                         dst[i * 128:(i + 1) * 128, c0:c0 + cc], cc))
            self.job_end[name] = len(self.tiles) - 1
        self.n = len(self.tiles)
        self.issued = 0
        self.stored = 0
        self.ld_marks = []
        self.st_marks = []
        self.job_marks = {}

    def mark(self, name):
        assert name in self.job_marks, (name, self.stored)
        return self.job_marks[name]

    def feed(self, gp, k):
        em = self.em
        for _ in range(k):
            if self.stored >= self.n:
                return
            while self.issued < min(self.n, self.stored + self.depth):
                i = self.issued
                if i >= self.depth:
                    gp.wait_ge(em.gso.h, self.st_marks[i - self.depth])
                s, d, c = self.tiles[i]
                em.gld.inc(gp.dma_start(
                    out=self.buf[:, i % self.depth, 0:c], in_=s))
                self.ld_marks.append(em.gld.v)
                self.issued += 1
            i = self.stored
            gp.wait_ge(em.gld.h, self.ld_marks[i])
            s, d, c = self.tiles[i]
            em.gso.inc(gp.dma_start(
                out=d, in_=self.buf[:, i % self.depth, 0:c]))
            self.st_marks.append(em.gso.v)
            for nm, endi in self.job_end.items():
                if endi == i:
                    self.job_marks[nm] = em.gso.v
            self.stored += 1

    def finish(self, gp, ag_list):
        em = self.em
        while self.stored < self.n:
            self.feed(gp, 1)
        gp.wait_ge(em.gso.h, em.gso.v)
        for src, dst in ag_list:
            gp.collective_compute(
                "AllGather", ALU.bypass,
                replica_groups=[list(range(8))],
                ins=[src.opt()], outs=[dst.opt()],
            ).then_inc(em.cc.h)
            em.cc.v += 1


class Emitter:
    def __init__(self, nc, ctx):
        self.nc = nc
        self.ctx = ctx
        self.ld = Sem(ctx, nc, "ld")
        self.so = Sem(ctx, nc, "so")
        self.tr = Sem(ctx, nc, "tr")
        self.cc = Sem(ctx, nc, "cc")
        self.gld = Sem(ctx, nc, "gld")
        self.gso = Sem(ctx, nc, "gso")

    def sb(self, name, shape, dtype):
        return self.ctx.enter_context(self.nc.sbuf_tensor(name, shape, dtype))

    def load(self, pairs, eng="sync"):
        ld = self.ld
        with self.nc.Block() as blk:
            def body(s):
                for dst, src in pairs:
                    ld.inc(s.dma_start(out=dst, in_=src))
                s.wait_ge(ld.h, ld.v)
            getattr(blk, eng)(body)

    def store(self, pairs, eng="sync"):
        so = self.so
        with self.nc.Block() as blk:
            def body(s):
                for dst, src in pairs:
                    so.inc(s.dma_start(out=dst, in_=src))
                s.wait_ge(so.h, so.v)
            getattr(blk, eng)(body)

    def cast_to_bf16_dram(self, src_f32, dst_bf, rows, cols, buf):
        gld, gso = self.gld, self.gso
        n = rows // 128
        with self.nc.Block() as blk:
            @blk.gpsimd
            def _(gp):
                ldm, stm = [], []
                for i in range(min(4, n)):
                    gld.inc(gp.dma_start(out=buf[:, i % 4, 0:cols],
                                         in_=src_f32[i * 128:(i + 1) * 128, :]))
                    ldm.append(gld.v)
                for i in range(n):
                    gp.wait_ge(gld.h, ldm[i])
                    gso.inc(gp.dma_start(out=dst_bf[i * 128:(i + 1) * 128, :],
                                         in_=buf[:, i % 4, 0:cols]))
                    stm.append(gso.v)
                    j = i + 4
                    if j < n:
                        gp.wait_ge(gso.h, stm[i])
                        gld.inc(gp.dma_start(
                            out=buf[:, j % 4, 0:cols],
                            in_=src_f32[j * 128:(j + 1) * 128, :]))
                        ldm.append(gld.v)
                gp.wait_ge(gso.h, gso.v)

    def load_wT(self, buf, src_bf, rows=D, nchunk=8, pre=None):
        tr = self.tr
        with self.nc.Block() as blk:
            @blk.sync
            def _(s):
                if pre is not None and pre[0] is not None:
                    s.wait_ge(pre[0], pre[1])
                for c in range(nchunk):
                    tr.inc(s.dma_start(out=buf[:, c, 0:rows],
                                       in_=src_bf[0:rows, c * 128:(c + 1) * 128],
                                       transpose=True))
                s.wait_ge(tr.h, tr.v)

    def layernorm(self, tile_srcs, sc, epscol=None, lo_dsts=None, feeder=None):
        nc = self.nc
        ld, so = self.ld, self.so
        gso = self.gso
        xld, xnf, xnb, sq, st = (sc[k] for k in ("xld", "xnf", "xnb", "sq", "st"))
        n = len(tile_srcs)
        for g0 in range(0, n, 8):
            grp = tile_srcs[g0:g0 + 8]
            m = len(grp)
            with nc.Block() as blk:
                @blk.sync
                def _(s, grp=grp):
                    for i, (src, _d) in enumerate(grp):
                        ld.inc(s.dma_start(out=xld[:, i, :], in_=src))

                @blk.scalar
                def _(s2, m=m):
                    s2.wait_ge(ld.h, ld.v)
                    for i in range(m):
                        s2.activation(sq[:, i % 2, :], xld[:, i, :], AF.Square,
                                      accum_out=st[:, i, 0:1])

                @blk.vector
                def _(v, m=m):
                    v.wait_ge(ld.h, ld.v)
                    for i in range(m):
                        v.reduce_sum(st[:, i, 1:2], xld[:, i, :], axis=AX.X)
            with nc.Block() as blk:
                @blk.vector
                def _(v, m=m):
                    for i in range(m):
                        v.tensor_scalar_mul(st[:, i, 2:3], st[:, i, 1:2], 1.0 / D)
                    for i in range(m):
                        v.tensor_scalar_mul(st[:, i, 3:4], st[:, i, 0:1], 1.0 / D)
                    for i in range(m):
                        v.tensor_tensor(st[:, i, 4:5], st[:, i, 2:3],
                                        st[:, i, 2:3], op=ALU.mult)
                    for i in range(m):
                        v.tensor_tensor(st[:, i, 3:4], st[:, i, 3:4],
                                        st[:, i, 4:5], op=ALU.subtract)
            with nc.Block() as blk:
                @blk.scalar
                def _(s2, m=m):
                    for i in range(m):
                        s2.activation(st[:, i, 5:6], st[:, i, 3:4], AF.Sqrt,
                                      bias=epscol[:])
            with nc.Block() as blk:
                @blk.vector
                def _(v, m=m):
                    for i in range(m):
                        v.reciprocal(st[:, i, 6:7], st[:, i, 5:6])
            with nc.Block() as blk:
                @blk.vector
                def _(v, m=m):
                    for i in range(m):
                        v.tensor_scalar(xnf[:, i, :], xld[:, i, :], st[:, i, 2:3],
                                        st[:, i, 6:7], op0=ALU.subtract,
                                        op1=ALU.mult)
            with nc.Block() as blk:
                @blk.gpsimd
                def _(gp, grp=grp):
                    for i in range(len(grp)):
                        gp.tensor_copy(xnb[:, i, :], xnf[:, i, :])
            lo_grp = (lo_dsts[g0:g0 + 8] if lo_dsts is not None
                      else [None] * len(grp))
            if any(d is not None for d in lo_grp):
                xlo = sc["xlo"]
                with nc.Block() as blk:
                    @blk.vector
                    def _(v, grp=grp, lo_grp=lo_grp):
                        for i in range(len(grp)):
                            if lo_grp[i] is not None:
                                v.tensor_tensor(xlo[:, i, :], xnf[:, i, :],
                                                xnb[:, i, :], op=ALU.subtract)
            with nc.Block() as blk:
                @blk.gpsimd
                def _(gp, grp=grp, lo_grp=lo_grp):
                    for i, (_s, dst) in enumerate(grp):
                        gso.inc(gp.dma_start(out=dst, in_=xnb[:, i, :]))
                    for i, d in enumerate(lo_grp):
                        if d is not None:
                            gso.inc(gp.dma_start(out=d, in_=sc["xlo"][:, i, :]))
                    if feeder is not None:
                        feeder.feed(gp, 2)
                    gp.wait_ge(gso.h, gso.v)


def build_l0(debug=()):
    nc = bass.Bass("TRN2", target_bir_lowering=False)
    io, scr = {}, {}

    def ein(nm, shape, dt=F32):
        io[nm] = nc.dram_tensor(nm, shape, dt, kind="ExternalInput").ap()

    def eout(nm, shape, dt=F32):
        io[nm] = nc.dram_tensor(nm, shape, dt, kind="ExternalOutput").ap()

    def dscr(nm, shape, dt=BF16):
        scr[nm] = nc.dram_tensor(nm, shape, dt)

    ein("x_own", [SH, D]); ein("x_eg", [S, D])
    ein("wq", [D, D]); ein("wk", [D, D]); ein("wv", [D, D]); ein("wo", [D, D])
    ein("wgate", [E, D])
    ein("win_own", [HID0, D]); ein("wout_own", [D, HID0])
    ein("cosq", [128, SH], FP16); ein("sinq", [128, SH], FP16)
    ein("cosk", [128, S], FP16); ein("sink", [128, S], FP16)
    ein("posq", [1, SH]); ein("posk", [128, 32])
    ein("identf", [128, 128])
    eout("x1_own", [SH, D])

    dscr("xn_eg_bf", [S, D]); dscr("xn_own_bf", [SH, D])
    dscr("xn_own_lo", [SH, D])
    dscr("wq_bf", [D, D]); dscr("wk_bf", [D, D]); dscr("wv_bf", [D, D])
    dscr("wo_bf", [D, D]); dscr("wout_bf", [D, HID0])
    dscr("kT_dram", [128, 8, S]); dscr("vS_dram", [128, 32, H * 65])
    dscr("prod_own", [D, D]); dscr("prod_all", [E * D, D])
    dscr("g_dram", [D, SH]); dscr("abs_dram", [4, 32, 128, 512], FP16)
    dscr("grow_dram", [E, SH], F32)

    dbg = {}
    for nm, shape, dt in [
        ("xn_eg", [S, D], BF16), ("qT", [128, 8, SH], BF16),
        ("kT", [128, 8, S], BF16), ("gates", [128, 16, E], F32),
        ("g", [D, SH], BF16), ("sT", [128, 8, SH], F32),
        ("prods", [E * D, D], BF16), ("cb", [128, 16], F32),
        ("aoT", [128, 8, SH], BF16),
    ]:
        if nm in debug:
            dbg[nm] = nc.dram_tensor("dbg_" + nm, shape, dt,
                                     kind="ExternalOutput").ap()

    scr = {k: v.ap() for k, v in scr.items()}
    with contextlib.ExitStack() as ctx:
        em = Emitter(nc, ctx)
        _emit_l0(em, io, scr, dbg)
    return nc


def _emit_l0(em, io, scr, dbg, x1_dst=None, feeder=None):
    nc, ctx, sb = em.nc, em.ctx, em.sb
    if x1_dst is None:
        x1_dst = io["x1_own"]

    cbias = sb("cbias", [128, 16], F32)
    identf_sb = sb("identf_sb", [128, 128], F32)
    identbf = sb("identbf", [128, 128], BF16)
    ones_row = sb("ones_row", [1, 128], BF16)
    onesf_row = sb("onesf_row", [1, 128], F32)
    onescol = sb("onescol", [128, 1], BF16)
    posk_sb = sb("posk_sb", [128, 32], F32)

    epscol = sb("epscol", [128, 1], F32)
    em.load([(identf_sb[:], io["identf"]), (posk_sb[:], io["posk"])])
    with nc.Block() as blk:
        @blk.gpsimd
        def _(gp):
            gp.memset(epscol[:], EPS)
            gp.memset(ones_row[:], 1.0)
            gp.memset(onesf_row[:], 1.0)
            gp.memset(onescol[:], 1.0)
            gp.tensor_copy(identbf[:], identf_sb[:])

    if feeder is None:
        with contextlib.ExitStack() as pc:
            wcast = pc.enter_context(
                nc.sbuf_tensor("wcast", [128, 4, HID0], BF16))
            for srcn, dstn, cols in [("wq", "wq_bf", D), ("wk", "wk_bf", D),
                                     ("wv", "wv_bf", D), ("wo", "wo_bf", D),
                                     ("wout_own", "wout_bf", HID0)]:
                em.cast_to_bf16_dram(io[srcn], scr[dstn], D, cols, wcast)

    with contextlib.ExitStack() as pc:
        lnscr = {
            "xld": pc.enter_context(nc.sbuf_tensor("ln_xld", [128, 8, D], F32)),
            "xnf": pc.enter_context(nc.sbuf_tensor("ln_xnf", [128, 8, D], F32)),
            "xnb": pc.enter_context(nc.sbuf_tensor("ln_xnb", [128, 8, D], BF16)),
            "sq": pc.enter_context(nc.sbuf_tensor("ln_sq", [128, 2, D], F32)),
            "st": pc.enter_context(nc.sbuf_tensor("ln_st", [128, 8, 8], F32)),
            "xlo": pc.enter_context(nc.sbuf_tensor("ln_xlo", [128, 8, D], BF16)),
        }
        tiles = [(io["x_eg"][t * 128:(t + 1) * 128, :],
                  scr["xn_eg_bf"][t * 128:(t + 1) * 128, :]) for t in range(32)]
        tiles += [(io["x_own"][t * 128:(t + 1) * 128, :],
                   scr["xn_own_bf"][t * 128:(t + 1) * 128, :]) for t in range(16)]
        lo_dsts = [None] * 32 + [scr["xn_own_lo"][t * 128:(t + 1) * 128, :]
                                 for t in range(16)]
        em.layernorm(tiles, lnscr, epscol, lo_dsts, feeder=feeder)

    if "xn_eg" in dbg:
        em.store([(dbg["xn_eg"], scr["xn_eg_bf"])])

    with contextlib.ExitStack() as scopeA:
        qT = scopeA.enter_context(nc.sbuf_tensor("qT", [128, 8, SH], BF16))

        with contextlib.ExitStack() as scopeB:
            xnT_own = scopeB.enter_context(
                nc.sbuf_tensor("xnT_own", [128, 8, SH], BF16))
            em.load_wT(xnT_own, scr["xn_own_bf"], rows=SH)

            with contextlib.ExitStack() as scopeC:
                xnT_eg = scopeC.enter_context(
                    nc.sbuf_tensor("xnT_eg", [128, 8, S], BF16))
                em.load_wT(xnT_eg, scr["xn_eg_bf"], rows=S)
                _l0_projections(em, io, scr, dbg, qT, xnT_own, xnT_eg,
                                cbias, onescol, onesf_row, feeder)

            _l0_gating(em, io, scr, dbg, xnT_own, identbf, identf_sb)
            _l0_products(em, io, scr, dbg, feeder)
            _l0_apply(em, io, scr, dbg, xnT_own, onesf_row, feeder)

        with contextlib.ExitStack() as pc:
            ps_q = pc.enter_context(nc.psum_tensor("ps_q", [128, 512], F32))
            dstf = pc.enter_context(nc.sbuf_tensor("dstf", [128, 4, 512], F32))
            absb = pc.enter_context(nc.sbuf_tensor("absb", [128, 4, 512], FP16))
            posq_sb = pc.enter_context(nc.sbuf_tensor("posq_sb", [1, SH], F32))
            em.load([(posq_sb[:], io["posq"])])
            for qb in range(4):
                with nc.Block() as blk:
                    @blk.tensor
                    def _(t, qb=qb):
                        t.matmul(ps_q[:], onesf_row[:],
                                 posq_sb[:, qb * 512:(qb + 1) * 512],
                                 start=True, stop=True)
                for k0 in range(0, 32, 4):
                    with nc.Block() as blk:
                        @blk.vector
                        def _(v, k0=k0):
                            for i in range(4):
                                v.tensor_scalar(dstf[:, i, :], ps_q[:],
                                                posk_sb[:, k0 + i:k0 + i + 1],
                                                None, op0=ALU.subtract)
                    with nc.Block() as blk:
                        @blk.scalar
                        def _(s2, k0=k0):
                            for i in range(4):
                                s2.activation(absb[:, i, :], dstf[:, i, :],
                                              AF.Abs)
                    em.store([(scr["abs_dram"][qb, k0 + i, :, :], absb[:, i, :])
                              for i in range(4)])

        attn_oT = scopeA.enter_context(
            nc.sbuf_tensor("attn_oT", [128, 8, SH], BF16))
        _attention_l0(em, scr, qT, attn_oT, cbias, onesf_row, feeder)
        if "aoT" in dbg:
            em.store([(dbg["aoT"], attn_oT[:])])
        if feeder is not None:
            with nc.Block() as blk:
                @blk.gpsimd
                def _(gp):
                    feeder.finish(gp, [(scr["w1_bf"], scr["w1_all"]),
                                       (scr["w2_bf"], scr["w2_all"]),
                                       (scr["w3_bf"], scr["w3_all"])])

        with contextlib.ExitStack() as pc:
            psb = lambda nm, shp: pc.enter_context(nc.psum_tensor(nm, shp, F32))
            sbp = lambda nm, shp, dt: pc.enter_context(
                nc.sbuf_tensor(nm, shp, dt))
            wT_o = sbp("wT_o", [128, 8, D], BF16)
            s_T = sbp("s_T", [128, 8, SH], F32)
            gT_sb = sbp("gT_sb", [128, 8, SH], BF16)
            ps_wo = psb("ps_wo", [128, 4, 512])
            em.load_wT(wT_o, scr["wo_bf"],
                       pre=(None if feeder is None else
                            (em.gso.h, feeder.mark("wo"))))
            for tb in range(4):
                for dg in range(2):
                    with nc.Block() as blk:
                        @blk.tensor
                        def _(t, tb=tb, dg=dg):
                            for doi in range(4):
                                do = dg * 4 + doi
                                for dc in range(8):
                                    t.matmul(ps_wo[:, doi, :],
                                             wT_o[:, dc, do * 128:(do + 1) * 128],
                                             attn_oT[:, dc,
                                                     tb * 512:(tb + 1) * 512],
                                             start=(dc == 0), stop=(dc == 7))
                    with nc.Block() as blk:
                        @blk.vector
                        def _(v, tb=tb, dg=dg):
                            for doi in range(4):
                                do = dg * 4 + doi
                                v.tensor_copy(
                                    s_T[:, do, tb * 512:(tb + 1) * 512],
                                    ps_wo[:, doi, :])
            if "sT" in dbg:
                em.store([(dbg["sT"], s_T[:])])

            em.load([(gT_sb[:],
                      scr["g_dram"].rearrange("(a p) n -> p a n", p=128))])
            ps_tr = psb("ps_tr", [128, 8, 128])
            ut = sbp("ut", [128, 8, 128], F32)
            xoutf = sbp("xoutf", [128, 2, D], F32)
            xin2 = sbp("xin2", [128, 2, D], F32)
            for tt in range(16):
                with nc.Block() as blk:
                    @blk.sync
                    def _(s, tt=tt):
                        em.ld.inc(s.dma_start(
                            out=xin2[:, tt % 2, :],
                            in_=io["x_own"][tt * 128:(tt + 1) * 128, :]))

                    @blk.vector
                    def _(v, tt=tt):
                        for dc in range(8):
                            v.tensor_tensor(
                                ut[:, dc, :],
                                s_T[:, dc, tt * 128:(tt + 1) * 128],
                                gT_sb[:, dc, tt * 128:(tt + 1) * 128],
                                op=ALU.mult)
                with nc.Block() as blk:
                    @blk.tensor
                    def _(t, tt=tt):
                        for dc in range(8):
                            t.transpose(ps_tr[:, dc, :], ut[:, dc, :],
                                        identf_sb[:])
                with nc.Block() as blk:
                    @blk.vector
                    def _(v, tt=tt):
                        v.wait_ge(em.ld.h, em.ld.v)
                        for dc in range(8):
                            v.tensor_tensor(
                                xoutf[:, tt % 2, dc * 128:(dc + 1) * 128],
                                ps_tr[:, dc, :],
                                xin2[:, tt % 2, dc * 128:(dc + 1) * 128],
                                op=ALU.add)
                em.store([(x1_dst[tt * 128:(tt + 1) * 128, :],
                           xoutf[:, tt % 2, :])])


def _l0_projections(em, io, scr, dbg, qT, xnT_own, xnT_eg, cbias, onescol,
                    onesf_row, feeder=None):
    nc = em.nc
    with contextlib.ExitStack() as pc:
        psb = lambda nm, shp: pc.enter_context(nc.psum_tensor(nm, shp, F32))
        sbp = lambda nm, shp, dt: pc.enter_context(nc.sbuf_tensor(nm, shp, dt))
        wT = sbp("wT", [128, 8, D], BF16)
        rc = sbp("rc", [128, S], FP16)
        rs = sbp("rs", [128, S], FP16)
        ps_pr = psb("ps_pr", [128, 4, 512])
        ps_n = psb("ps_n", [128, 4, 512])
        praw = sbp("praw", [128, 2, 512], F32)
        prot = sbp("prot", [128, 2, 512], F32)
        pcos = sbp("pcos", [128, 2, 512], F32)
        psin = sbp("psin", [128, 2, 512], F32)
        nsq = sbp("nsq", [128, 2, 512], BF16)
        kT_g = sbp("kT_g", [128, 2, S], BF16)
        qmx = sbp("qmx", [1, H, 4], F32)
        kmx = sbp("kmx", [1, H, 8], F32)
        cbs = sbp("cbs", [1, 6 * H], F32)

        r_mm = Sem(em.ctx, nc, "r_mm")
        r_vc = Sem(em.ctx, nc, "r_vc")
        r_gp = Sem(em.ctx, nc, "r_gp")
        r_sq = Sem(em.ctx, nc, "r_sq")
        r_vo = Sem(em.ctx, nc, "r_vo")
        r_nm = Sem(em.ctx, nc, "r_nm")
        r_mx = Sem(em.ctx, nc, "r_mx")

        def rope_block(xnT, hp, chunks, out_fn, ropec, ropes, mx, mslot0):
            nb = len(chunks)
            b = dict(mm=r_mm.v, vc=r_vc.v, gp=r_gp.v, sq=r_sq.v,
                     vo=r_vo.v, nm=r_nm.v, mx=r_mx.v)
            r_mm.v += nb
            r_vc.v += nb
            r_gp.v += nb
            r_sq.v += nb
            r_vo.v += nb
            r_nm.v += nb
            r_mx.v += nb
            with nc.Block() as blk:
                @blk.tensor
                def _(t, hp=hp, chunks=chunks, b=b):
                    def norm_mm(j):
                        t.wait_ge(r_sq.h, b["sq"] + j + 1)
                        if j >= 2:
                            t.wait_ge(r_mx.h, b["mx"] + j - 1)
                        sl = 2 * (j % 2)
                        t.matmul(ps_n[0:1, sl, :], onescol[0:64, :],
                                 nsq[0:64, j % 2, :], start=True, stop=True)
                        ins = t.matmul(ps_n[0:1, sl + 1, :], onescol[64:128, :],
                                       nsq[64:128, j % 2, :], start=True,
                                       stop=True, tile_position=(64, 0))
                        ins.then_inc(r_nm.h, 1)

                    for i, c4 in enumerate(chunks):
                        if i >= 2:
                            t.wait_ge(r_vc.h, b["vc"] + i - 1)
                        for dc in range(8):
                            ins = t.matmul(ps_pr[:, i % 2, :],
                                           wT[:, dc, hp * 128:(hp + 1) * 128],
                                           xnT[:, dc,
                                               c4 * 512:(c4 + 1) * 512],
                                           start=(dc == 0), stop=(dc == 7))
                        ins.then_inc(r_mm.h, 1)
                        if i >= 1:
                            norm_mm(i - 1)
                    norm_mm(len(chunks) - 1)

                @blk.vector
                def _(v, hp=hp, chunks=chunks, b=b, mx=mx, mslot0=mslot0):
                    def stage2(j):
                        v.wait_ge(r_gp.h, b["gp"] + j + 1)
                        c = chunks[j] * 512
                        ins = v.tensor_tensor(out_fn(c), pcos[:, j % 2, :],
                                              psin[:, j % 2, :], op=ALU.add)
                        ins.then_inc(r_vo.h, 1)

                    def stage3(j):
                        v.wait_ge(r_nm.h, b["nm"] + j + 1)
                        sl = 2 * (j % 2)
                        ms = mslot0 + j
                        v.reduce_max(mx[:, 2 * hp, ms:ms + 1],
                                     ps_n[0:1, sl, :], axis=AX.X)
                        ins = v.reduce_max(mx[:, 2 * hp + 1, ms:ms + 1],
                                           ps_n[0:1, sl + 1, :], axis=AX.X)
                        ins.then_inc(r_mx.h, 1)

                    for i, c4 in enumerate(chunks):
                        v.wait_ge(r_mm.h, b["mm"] + i + 1)
                        if i >= 2:
                            v.wait_ge(r_sq.h, b["sq"] + i - 1)
                        c = c4 * 512
                        v.tensor_copy(praw[:, i % 2, :], ps_pr[:, i % 2, :])
                        v.stream_shuffle(prot[:, i % 2, :], praw[:, i % 2, :],
                                         SWAPMASK)
                        ins = v.tensor_tensor(pcos[:, i % 2, :],
                                              praw[:, i % 2, :],
                                              ropec[:, c:c + 512],
                                              op=ALU.mult)
                        ins.then_inc(r_vc.h, 1)
                        if i >= 1:
                            stage2(i - 1)
                        if i >= 2:
                            stage3(i - 2)
                    nb = len(chunks)
                    stage2(nb - 1)
                    if nb >= 2:
                        stage3(nb - 2)
                    stage3(nb - 1)

                @blk.gpsimd
                def _(gp, chunks=chunks, b=b):
                    for i, c4 in enumerate(chunks):
                        gp.wait_ge(r_vc.h, b["vc"] + i + 1)
                        if i >= 2:
                            gp.wait_ge(r_vo.h, b["vo"] + i - 1)
                        c = c4 * 512
                        ins = gp.tensor_mul(psin[:, i % 2, :],
                                            prot[:, i % 2, :],
                                            ropes[:, c:c + 512])
                        ins.then_inc(r_gp.h, 1)
                    if feeder is not None:
                        feeder.feed(gp, 1)

                @blk.scalar
                def _(s2, chunks=chunks, b=b):
                    for i, c4 in enumerate(chunks):
                        s2.wait_ge(r_vc.h, b["vc"] + i + 1)
                        if i >= 2:
                            s2.wait_ge(r_nm.h, b["nm"] + i - 1)
                        ins = s2.activation(nsq[:, i % 2, :],
                                            praw[:, i % 2, :], AF.Square)
                        ins.then_inc(r_sq.h, 1)

        em.load_wT(wT, scr["wq_bf"],
                   pre=(None if feeder is None else
                        (em.gso.h, feeder.mark("wq"))))
        em.load([(rc[:, 0:SH], io["cosq"]), (rs[:, 0:SH], io["sinq"])])
        for hp in range(8):
            rope_block(xnT_own, hp, [0, 1, 2, 3],
                       lambda c, hp=hp: qT[:, hp, c:c + 512], rc, rs,
                       qmx, 0)
        if "qT" in dbg:
            em.store([(dbg["qT"], qT[:])])

        em.load_wT(wT, scr["wk_bf"],
                   pre=(None if feeder is None else
                        (em.gso.h, feeder.mark("wk"))))
        em.load([(rc[:], io["cosk"]), (rs[:], io["sink"])])
        for hp in range(8):
            lhp = hp % 2
            for h4 in range(2):
                rope_block(xnT_eg, hp, [4 * h4 + i for i in range(4)],
                           lambda c, lhp=lhp: kT_g[:, lhp, c:c + 512], rc, rs,
                           kmx, 4 * h4)
            if hp % 2 == 1:
                em.store([(scr["kT_dram"][:, hp - 1, :], kT_g[:, 0, :]),
                          (scr["kT_dram"][:, hp, :], kT_g[:, 1, :])])
        if "kT" in dbg:
            em.store([(dbg["kT"], scr["kT_dram"])])

        em.load_wT(wT, scr["wv_bf"],
                   pre=(None if feeder is None else
                        (em.gso.h, feeder.mark("wv"))))
        vS_tile = sbp("vS_tile", [128, 2, H * 65], BF16)
        with nc.Block() as blk:
            @blk.gpsimd
            def _(gp):
                for sl in range(2):
                    gp.memset(vS_tile[:, sl, :].rearrange(
                        "p (h c) -> p h c", c=65)[:, :, 64:65], 1.0)
        rv_mm = Sem(em.ctx, nc, "r_pvmm")
        rv_cp = Sem(em.ctx, nc, "r_pvcp")
        vst_mk = {}
        rv_mm.v += 32
        rv_cp.v += 32
        with nc.Block() as blk:
            @blk.tensor
            def _(t):
                for tt in range(32):
                    if tt >= 2:
                        t.wait_ge(rv_cp.h, tt - 1)
                    ins = None
                    for vb in range(2):
                        sl = (tt % 2) * 2 + vb
                        for dc in range(8):
                            ins = t.matmul(ps_pr[:, sl, :],
                                           xnT_eg[:, dc,
                                                  tt * 128:(tt + 1) * 128],
                                           wT[:, dc, vb * 512:(vb + 1) * 512],
                                           start=(dc == 0), stop=(dc == 7))
                    ins.then_inc(rv_mm.h, 1)

            @blk.sync
            def _(s):
                for tt in range(32):
                    s.wait_ge(rv_cp.h, tt + 1)
                    em.so.inc(s.dma_start(
                        out=scr["vS_dram"][:, tt, :],
                        in_=vS_tile[:, tt % 2, :]))
                    vst_mk[tt] = em.so.v
                s.wait_ge(em.so.h, em.so.v)

            @blk.vector
            def _(v):
                for tt in range(32):
                    v.wait_ge(rv_mm.h, tt + 1)
                    if tt >= 2:
                        v.wait_ge(em.so.h, vst_mk[tt - 2])
                    ins = None
                    for vb in range(2):
                        sl = (tt % 2) * 2 + vb
                        dst = vS_tile[:, tt % 2, :].rearrange(
                            "p (h c) -> p h c", c=65)[:, vb * 8:(vb + 1) * 8,
                                                      0:64]
                        ins = v.tensor_copy(dst, ps_pr[:, sl, :].rearrange(
                            "p (h c) -> p h c", c=64))
                    ins.then_inc(rv_cp.h, 1)

            if feeder is not None:
                @blk.gpsimd
                def _(gp):
                    feeder.feed(gp, 24)

        with nc.Block() as blk:
            @blk.vector
            def _(v):
                for h in range(H):
                    v.reduce_max(cbs[:, h:h + 1], qmx[:, h, :], axis=AX.X)
                for h in range(H):
                    v.reduce_max(cbs[:, H + h:H + h + 1], kmx[:, h, :],
                                 axis=AX.X)
        with nc.Block() as blk:
            @blk.vector
            def _(v):
                v.tensor_tensor(cbs[:, 2 * H:3 * H], cbs[:, 0:H],
                                cbs[:, H:2 * H], op=ALU.mult)
        with nc.Block() as blk:
            @blk.scalar
            def _(s2):
                s2.activation(cbs[:, 3 * H:4 * H], cbs[:, 2 * H:3 * H], AF.Sqrt)
        with nc.Block() as blk:
            @blk.tensor
            def _(t):
                t.matmul(ps_pr[:, 0, 0:16], onesf_row[:], cbs[:, 3 * H:4 * H],
                         start=True, stop=True)
        with nc.Block() as blk:
            @blk.vector
            def _(v):
                v.tensor_scalar_mul(cbias[:], ps_pr[:, 0, 0:16], -0.125)
        if "cb" in dbg:
            em.store([(dbg["cb"], cbias[:])])


def _l0_gating(em, io, scr, dbg, xnT_own, identbf, identf_sb):
    nc = em.nc
    with contextlib.ExitStack() as pc:
        psb = lambda nm, shp, dt=F32: pc.enter_context(
            nc.psum_tensor(nm, shp, dt))
        sbp = lambda nm, shp, dt: pc.enter_context(nc.sbuf_tensor(nm, shp, dt))
        wg_sb = sbp("wg_sb", [E, D], BF16)
        wg_f = sbp("wg_f", [E, D], F32)
        wg_lo = sbp("wg_lo", [E, D], BF16)
        wgT = sbp("wgT", [128, 8, 2 * E], BF16)
        xnT_lo = sbp("xnT_lo", [128, 8, SH], BF16)
        ps_t8 = psb("ps_t8", [128, E], BF16)
        ps_lg = psb("ps_lg", [128, 4, 512])
        gt_ps = psb("gt_ps", [128, 2, 512])
        lgt = sbp("lgt", [128, 16, E], F32)
        g2 = sbp("g2", [128, 16, 14], F32)
        eqs = sbp("eqs", [128, 16, 2 * E], F32)
        gates = sbp("gates", [128, 16, E], F32)
        gatesb = sbp("gatesb", [128, 16, E], F32)
        gt_sb = sbp("gt_sb", [E, SH], F32)

        em.load_wT(xnT_lo, scr["xn_own_lo"], rows=SH)
        with nc.Block() as blk:
            @blk.gpsimd
            def _(gp):
                em.gld.inc(gp.dma_start(out=wg_sb[:], in_=io["wgate"]))
                em.gld.inc(gp.dma_start(out=wg_f[:], in_=io["wgate"]))
                gp.wait_ge(em.gld.h, em.gld.v)
        with nc.Block() as blk:
            @blk.vector
            def _(v):
                v.tensor_tensor(wg_lo[:], wg_f[:], wg_sb[:], op=ALU.subtract)
        for dc in range(8):
            for which, buf in ((0, wg_sb), (1, wg_lo)):
                with nc.Block() as blk:
                    @blk.tensor
                    def _(t, dc=dc, buf=buf):
                        t.transpose(ps_t8[:], buf[:, dc * 128:(dc + 1) * 128],
                                    identbf[0:E, 0:E])
                with nc.Block() as blk:
                    @blk.vector
                    def _(v, dc=dc, which=which):
                        v.tensor_copy(wgT[:, dc, which * E:(which + 1) * E],
                                      ps_t8[:])

        for t0 in range(0, 16, 4):
            with nc.Block() as blk:
                @blk.tensor
                def _(t, t0=t0):
                    for i in range(4):
                        tt = t0 + i
                        first = True
                        for xb, wslc in ((xnT_own, 0), (xnT_own, 1),
                                         (xnT_lo, 0), (xnT_lo, 1)):
                            for dc in range(8):
                                t.matmul(ps_lg[:, i, 0:E],
                                         xb[:, dc, tt * 128:(tt + 1) * 128],
                                         wgT[:, dc, wslc * E:(wslc + 1) * E],
                                         start=first, stop=(xb is xnT_lo and
                                                            wslc == 1 and
                                                            dc == 7))
                                first = False
            with nc.Block() as blk:
                @blk.vector
                def _(v, t0=t0):
                    for i in range(4):
                        v.tensor_copy(lgt[:, t0 + i, :], ps_lg[:, i, 0:E])

        _pend = []

        def vblock(fn):
            _pend.append(fn)

        def vflush():
            fns = list(_pend)
            del _pend[:]
            if not fns:
                return
            with nc.Block() as blk:
                @blk.vector
                def _(v):
                    for fn in fns:
                        fn(v)

        T16 = range(16)
        vblock(lambda v: [v.reduce_max(g2[:, t, 0:1], lgt[:, t, :], axis=AX.X)
                          for t in T16])
        vblock(lambda v: [v.tensor_scalar(eqs[:, t, 0:E], lgt[:, t, :],
                                          g2[:, t, 0:1], None,
                                          op0=ALU.is_equal) for t in T16])
        vblock(lambda v: [v.scalar_tensor_tensor(g2[:, t, 5:13],
                                                 eqs[:, t, 0:E], -1e9,
                                                 lgt[:, t, :], op0=ALU.mult,
                                                 op1=ALU.add) for t in T16])
        vblock(lambda v: [v.reduce_max(g2[:, t, 1:2], g2[:, t, 5:13],
                                       axis=AX.X) for t in T16])
        vblock(lambda v: [v.tensor_scalar(eqs[:, t, E:2 * E], g2[:, t, 5:13],
                                          g2[:, t, 1:2], None,
                                          op0=ALU.is_equal) for t in T16])
        vblock(lambda v: [v.tensor_tensor(g2[:, t, 2:3], g2[:, t, 1:2],
                                          g2[:, t, 0:1], op=ALU.subtract)
                          for t in T16])
        vflush()
        with nc.Block() as blk:
            @blk.scalar
            def _(s2):
                for t in T16:
                    s2.activation(g2[:, t, 3:4], g2[:, t, 2:3], AF.Exp)
        vblock(lambda v: [v.tensor_scalar_add(g2[:, t, 4:5], g2[:, t, 3:4],
                                              1.0) for t in T16])
        vblock(lambda v: [v.reciprocal(g2[:, t, 13:14], g2[:, t, 4:5])
                          for t in T16])
        vblock(lambda v: [v.tensor_tensor(g2[:, t, 4:5], g2[:, t, 3:4],
                                          g2[:, t, 13:14], op=ALU.mult)
                          for t in T16])
        vblock(lambda v: [v.tensor_scalar(gates[:, t, :], eqs[:, t, 0:E],
                                          g2[:, t, 13:14], None,
                                          op0=ALU.mult) for t in T16])
        vblock(lambda v: [v.scalar_tensor_tensor(gatesb[:, t, :],
                                                 eqs[:, t, E:2 * E],
                                                 g2[:, t, 4:5],
                                                 gates[:, t, :],
                                                 op0=ALU.mult, op1=ALU.add)
                          for t in T16])
        vflush()
        if "gates" in dbg:
            em.store([(dbg["gates"], gatesb[:])])

        g_tr = Sem(em.ctx, nc, "g0_tr")
        g_cp = Sem(em.ctx, nc, "g0_cp")
        g_tr.v += 16
        g_cp.v += 16
        with nc.Block() as blk:
            @blk.tensor
            def _(t):
                for tt in range(16):
                    if tt >= 2:
                        t.wait_ge(g_cp.h, tt - 1)
                    ins = t.transpose(gt_ps[0:E, tt % 2, 0:128],
                                      gatesb[:, tt, :], identf_sb[:])
                    ins.then_inc(g_tr.h, 1)

            @blk.vector
            def _(v):
                for tt in range(16):
                    v.wait_ge(g_tr.h, tt + 1)
                    ins = v.tensor_copy(gt_sb[:, tt * 128:(tt + 1) * 128],
                                        gt_ps[0:E, tt % 2, 0:128])
                    ins.then_inc(g_cp.h, 1)
        em.store([(scr["grow_dram"], gt_sb[:])])


def _l0_products(em, io, scr, dbg, feeder=None):
    nc = em.nc
    with contextlib.ExitStack() as pc:
        psb = lambda nm, shp: pc.enter_context(nc.psum_tensor(nm, shp, F32))
        sbp = lambda nm, shp, dt: pc.enter_context(nc.sbuf_tensor(nm, shp, dt))
        win_g = sbp("win_g", [128, 8, D], BF16)
        woutT_g = sbp("woutT_g", [128, 8, D], BF16)
        ps_prod = psb("ps_prod", [128, 8, 512])
        prodsb = sbp("prodsb", [128, 8, 512], BF16)
        for d2b in range(2):
            for wg in range(4):
                with nc.Block() as blk:
                    @blk.gpsimd
                    def _(gp, wg=wg):
                        for hc in range(8):
                            r = (wg * 8 + hc) * 128
                            em.gld.inc(gp.dma_start(
                                out=win_g[:, hc, :],
                                in_=io["win_own"][r:r + 128, :]))

                    @blk.sync
                    def _(s, wg=wg, d2b=d2b):
                        if feeder is not None and d2b == 0 and wg == 0:
                            s.wait_ge(em.gso.h, feeder.mark("wout_own"))
                        for hc in range(8):
                            c = (wg * 8 + hc) * 128
                            em.tr.inc(s.dma_start(
                                out=woutT_g[:, hc, :],
                                in_=scr["wout_bf"][0:D, c:c + 128],
                                transpose=True))
                with nc.Block() as blk:
                    @blk.tensor
                    def _(t, d2b=d2b, wg=wg):
                        t.wait_ge(em.gld.h, em.gld.v)
                        t.wait_ge(em.tr.h, em.tr.v)
                        for hc in range(8):
                            for dt_ in range(8):
                                t.matmul(ps_prod[:, dt_, :],
                                         win_g[:, hc,
                                               dt_ * 128:(dt_ + 1) * 128],
                                         woutT_g[:, hc,
                                                 d2b * 512:(d2b + 1) * 512],
                                         start=(wg == 0 and hc == 0),
                                         stop=(wg == 3 and hc == 7))
            with nc.Block() as blk:
                @blk.vector
                def _(v):
                    for dt_ in range(8):
                        v.tensor_copy(prodsb[:, dt_, :], ps_prod[:, dt_, :])
            em.store([(scr["prod_own"][dt_ * 128:(dt_ + 1) * 128,
                                       d2b * 512:(d2b + 1) * 512],
                       prodsb[:, dt_, :]) for dt_ in range(8)])
        with nc.Block() as blk:
            @blk.gpsimd
            def _(gp):
                gp.collective_compute(
                    "AllGather", ALU.bypass,
                    replica_groups=[list(range(8))],
                    ins=[scr["prod_own"].opt()],
                    outs=[scr["prod_all"].opt()],
                ).then_inc(em.cc.h)
                gp.wait_ge(em.cc.h, 1)
        if "prods" in dbg:
            em.store([(dbg["prods"], scr["prod_all"])])


def _l0_apply(em, io, scr, dbg, xnT_own, onesf_row, feeder=None):
    nc = em.nc
    with contextlib.ExitStack() as pc:
        sbp = lambda nm, shp, dt: pc.enter_context(nc.sbuf_tensor(nm, shp, dt))
        ps8 = pc.enter_context(nc.psum_tensor("ps_ap8", [128, 8, 512], F32))
        grow_tb = sbp("grow_tb", [1, E, 512], F32)
        ge8 = sbp("ge8", [128, 8, 512], F32)
        xe = sbp("xe2", [128, 2, 8, 512], BF16)
        pe_sb = sbp("pe_sb", [128, 2, 8, D], BF16)
        gsb = sbp("gsb", [128, 8, 512], BF16)
        t_ge = Sem(em.ctx, nc, "ap_tge")
        v_ge = Sem(em.ctx, nc, "ap_vge")
        v_xe = Sem(em.ctx, nc, "ap_vxe")
        t_mm = Sem(em.ctx, nc, "ap_tmm")
        s_si = Sem(em.ctx, nc, "ap_ssi")
        for tb in range(4):
            gw_mk = {}
            pe_mk = {}
            base_mm, base_xe = t_mm.v, v_xe.v
            ge0, vg0, si0 = t_ge.v, v_ge.v, s_si.v
            t_ge.v += 1
            v_ge.v += 1
            s_si.v += 1
            t_mm.v += E
            v_xe.v += E
            with nc.Block() as blk:
                @blk.sync
                def _(s, tb=tb, base_mm=base_mm, si0=si0):
                    for e in range(E):
                        em.ld.inc(s.dma_start(
                            out=grow_tb[0:1, e, :],
                            in_=scr["grow_dram"][e:e + 1,
                                                 tb * 512:(tb + 1) * 512]))
                    gw_mk[0] = em.ld.v
                    for e in range(E):
                        if e >= 2:
                            s.wait_ge(t_mm.h, base_mm + e - 1)
                        for dc in range(8):
                            em.ld.inc(s.dma_start(
                                out=pe_sb[:, e % 2, dc, :],
                                in_=scr["prod_all"][e * D + dc * 128:
                                                    e * D + (dc + 1) * 128,
                                                    :]))
                        pe_mk[e] = em.ld.v
                    s.wait_ge(s_si.h, si0 + 1)
                    for do in range(8):
                        em.so.inc(s.dma_start(
                            out=scr["g_dram"][do * 128:(do + 1) * 128,
                                              tb * 512:(tb + 1) * 512],
                            in_=gsb[:, do, :]))
                    s.wait_ge(em.so.h, em.so.v)

                @blk.tensor
                def _(t, tb=tb, base_mm=base_mm, base_xe=base_xe, vg0=vg0):
                    t.wait_ge(em.ld.h, gw_mk[0])
                    ins = None
                    for e in range(E):
                        ins = t.matmul(ps8[:, e, :], onesf_row[:],
                                       grow_tb[0:1, e, :],
                                       start=True, stop=True)
                    ins.then_inc(t_ge.h, 1)
                    t.wait_ge(v_ge.h, vg0 + 1)
                    for e in range(E):
                        t.wait_ge(em.ld.h, pe_mk[e])
                        t.wait_ge(v_xe.h, base_xe + e + 1)
                        ins = None
                        for dc in range(8):
                            for do in range(8):
                                ins = t.matmul(
                                    ps8[:, do, :],
                                    pe_sb[:, e % 2, dc,
                                          do * 128:(do + 1) * 128],
                                    xe[:, e % 2, dc, :],
                                    start=(e == 0 and dc == 0),
                                    stop=(e == E - 1 and dc == 7),
                                    skip_group_check=True)
                        ins.then_inc(t_mm.h, 1)

                @blk.vector
                def _(v, tb=tb, base_mm=base_mm, ge0=ge0):
                    v.wait_ge(t_ge.h, ge0 + 1)
                    ins = None
                    for e in range(E):
                        ins = v.tensor_copy(ge8[:, e, :], ps8[:, e, :])
                    ins.then_inc(v_ge.h, 1)
                    for e in range(E):
                        if e >= 2:
                            v.wait_ge(t_mm.h, base_mm + e - 1)
                        ins = None
                        for dc in range(8):
                            ins = v.tensor_tensor(
                                xe[:, e % 2, dc, :],
                                xnT_own[:, dc, tb * 512:(tb + 1) * 512],
                                ge8[:, e, :], op=ALU.mult)
                        ins.then_inc(v_xe.h, 1)

                @blk.scalar
                def _(s2, base_mm=base_mm):
                    s2.wait_ge(t_mm.h, base_mm + E)
                    ins = None
                    for do in range(8):
                        ins = s2.activation(gsb[:, do, :], ps8[:, do, :],
                                            AF.Silu)
                    ins.then_inc(s_si.h, 1)

                if feeder is not None:
                    @blk.gpsimd
                    def _(gp):
                        feeder.feed(gp, 16)
    if "g" in dbg:
        em.store([(dbg["g"], scr["g_dram"])])


def _attention_l0(em, scr, qT, attn_oT, cbias, onesf_row, feeder=None):
    nc = em.nc
    with contextlib.ExitStack() as pc:
        sbp = lambda nm, shp, dt: pc.enter_context(nc.sbuf_tensor(nm, shp, dt))
        kT_g = sbp("akT_g", [128, 2, S], BF16)
        v_g = sbp("av_g", [128, 32, 4 * 65], BF16)
        abs_sb = sbp("aabs", [128, 32, 512], FP16)
        sscr = sbp("asscr", [128, 4, 1024], F32)
        attnU = sbp("aattnU", [128, 4, 1024], BF16)
        zrow = sbp("azrow", [1, 4, 512], F32)
        rz = sbp("arz", [1, 4, 512], F32)
        bzs = sbp("abzs", [64, 4, 512], F32)
        ps_sc = pc.enter_context(nc.psum_tensor("ps_sc", [128, 4, 512], F32))
        ps_av = pc.enter_context(nc.psum_tensor("ps_av", [128, 4, 512], F32))

        s_mm = Sem(em.ctx, nc, "s_mm")
        s_al = Sem(em.ctx, nc, "s_al")
        s_ex = Sem(em.ctx, nc, "s_ex")
        s_av = Sem(em.ctx, nc, "s_av")
        a_rz = Sem(em.ctx, nc, "a_rz")
        a_bz = Sem(em.ctx, nc, "a_bz")
        a_cons = Sem(em.ctx, nc, "a_cons")

        kv_mk = {}
        abs_mk = {}
        em.load([(kT_g[:, 0, :], scr["kT_dram"][:, 0, :]),
                 (kT_g[:, 1, :], scr["kT_dram"][:, 1, :]),
                 (v_g[:], scr["vS_dram"][:, :, 0:260]),
                 (abs_sb[:],
                  scr["abs_dram"][0].rearrange("k p n -> p k n"))])
        kv_mk[0] = em.ld.v
        abs_mk[(0, 0, 0)] = abs_mk[(0, 0, 1)] = em.ld.v
        for g in range(4):
            cons_base = a_cons.v

            def band_kts(qb, hp):
                d_cut = 106.0 * 2.0 ** ((2 * hp + 1) / 2.0)
                gmax = int((d_cut - 1) // 128 + 1)
                lo, hi = 8 * qb - gmax, 8 * qb + 7 + gmax
                ktsA = [kt for kt in range(16) if lo <= 2 * kt <= hi]
                ktsB = [16 + i for i in range(16) if lo <= 2 * i + 1 <= hi]
                if (len(ktsA) + len(ktsB)) % 2 == 1:
                    extra = ([kt for kt in range(16) if kt not in ktsA] or
                             [None])
                    if ktsA[0] > 0:
                        ktsA.insert(0, ktsA[0] - 1)
                    elif ktsA[-1] < 15:
                        ktsA.append(ktsA[-1] + 1)
                    elif ktsB[0] > 16:
                        ktsB.insert(0, ktsB[0] - 1)
                    else:
                        ktsB.append(ktsB[-1] + 1)
                return ktsA + ktsB

            for qb in range(4):
                kts2 = [band_kts(qb, 2 * g + lhp) for lhp in range(2)]
                ns2 = [len(k) for k in kts2]
                bases = []
                for lhp in range(2):
                    bases.append((s_mm.v, s_al.v, s_ex.v, s_av.v))
                    s_mm.v += ns2[lhp]
                    s_al.v += ns2[lhp]
                    s_ex.v += ns2[lhp] // 2
                    s_av.v += ns2[lhp]
                rz0, bz0 = a_rz.v, a_bz.v
                a_rz.v += 1
                a_bz.v += 1
                a_cons.v += 2
                with nc.Block() as blk:
                    @blk.sync
                    def _(s, g=g, qb=qb, cons_base=cons_base,
                          av_end=s_av.v):
                        if qb < 3:
                            for h in range(2):
                                s.wait_ge(a_cons.h,
                                          cons_base + 2 * qb + h + 1)
                                em.ld.inc(s.dma_start(
                                    out=abs_sb[:, h * 16:(h + 1) * 16, :],
                                    in_=scr["abs_dram"][
                                        qb + 1, h * 16:(h + 1) * 16].rearrange(
                                        "k p n -> p k n")))
                                abs_mk[(g, qb + 1, h)] = em.ld.v
                        elif g < 3:
                            s.wait_ge(s_av.h, av_end)
                            s.wait_ge(a_cons.h, cons_base + 8)
                            for dst, srcv in (
                                    (kT_g[:, 0, :],
                                     scr["kT_dram"][:, 2 * g + 2, :]),
                                    (kT_g[:, 1, :],
                                     scr["kT_dram"][:, 2 * g + 3, :]),
                                    (v_g[:],
                                     scr["vS_dram"][:, :, (g + 1) * 260:
                                                    (g + 2) * 260]),
                                    (abs_sb[:],
                                     scr["abs_dram"][0].rearrange(
                                         "k p n -> p k n"))):
                                em.ld.inc(s.dma_start(out=dst, in_=srcv))
                            kv_mk[g + 1] = em.ld.v
                            abs_mk[(g + 1, 0, 0)] = em.ld.v
                            abs_mk[(g + 1, 0, 1)] = em.ld.v

                    def av_pair(t, p, lhp, kts):
                        sp = p % 2
                        ins = None
                        for half_ in range(2):
                            ki = 2 * p + half_
                            kt = kts[ki]
                            for side in range(2):
                                ins = t.matmul(
                                    ps_av[0:65, 2 * lhp + side, :],
                                    v_g[:, kt,
                                        (2 * lhp + side) * 65:
                                        (2 * lhp + side) * 65 + 65],
                                    attnU[:, side * 2 + sp,
                                          half_ * 512:(half_ + 1) * 512],
                                    start=(ki == 0), stop=(ki == len(kts) - 1),
                                    skip_group_check=True)
                            ins.then_inc(s_av.h, 1)

                    @blk.tensor
                    def _(t, g=g, qb=qb, bases=bases, rz0=rz0, kts2=kts2):
                        if qb == 0:
                            t.wait_ge(em.ld.h, kv_mk[g])
                        for lhp in range(2):
                            mm0, al0, ex0, av0 = bases[lhp]
                            kts = kts2[lhp]
                            for ki, kt in enumerate(kts):
                                if ki >= 2:
                                    t.wait_ge(s_al.h, al0 + ki - 1)
                                sl = (ki % 2) * 2
                                t.matmul(ps_sc[:, sl, :],
                                         kT_g[0:64, lhp,
                                              kt * 128:(kt + 1) * 128],
                                         qT[0:64, 2 * g + lhp,
                                            qb * 512:(qb + 1) * 512],
                                         start=True, stop=True,
                                         tile_position=(0, 0))
                                ins = t.matmul(ps_sc[:, sl + 1, :],
                                               kT_g[64:128, lhp,
                                                    kt * 128:(kt + 1) * 128],
                                               qT[64:128, 2 * g + lhp,
                                                  qb * 512:(qb + 1) * 512],
                                               start=True, stop=True,
                                               tile_position=(64, 0))
                                ins.then_inc(s_mm.h, 1)
                                if ki % 2 == 1 and ki >= 3:
                                    p = (ki - 1) // 2
                                    t.wait_ge(s_ex.h, ex0 + p)
                                    av_pair(t, p - 1, lhp, kts)
                            t.wait_ge(s_ex.h, ex0 + len(kts) // 2)
                            av_pair(t, len(kts) // 2 - 1, lhp, kts)
                        t.wait_ge(a_rz.h, rz0 + 1)
                        ins = None
                        for i in range(4):
                            ins = t.matmul(ps_sc[0:64, i, :],
                                           onesf_row[:, 0:64], rz[0:1, i, :],
                                           start=True, stop=True)
                        ins.then_inc(a_bz.h, 1)

                    @blk.vector
                    def _(v, g=g, qb=qb, bases=bases, bz0=bz0, kts2=kts2):
                        for lhp in range(2):
                            mm0, al0, ex0, av0 = bases[lhp]
                            kts = kts2[lhp]
                            hA, hB = 2 * (2 * g + lhp), 2 * (2 * g + lhp) + 1
                            slA = -8.0 * 2.0 ** (-0.5 * hA)
                            slB = -8.0 * 2.0 ** (-0.5 * hB)
                            half1_first = min(ki for ki, kt in enumerate(kts)
                                              if kt >= 16)
                            for ki, kt in enumerate(kts):
                                p = ki // 2
                                if lhp == 0 and ki in (0, half1_first):
                                    v.wait_ge(em.ld.h,
                                              abs_mk[(g, qb, kt // 16)])
                                v.wait_ge(s_mm.h, mm0 + ki + 1)
                                if p >= 2:
                                    v.wait_ge(s_ex.h, ex0 + p - 1)
                                sl = (ki % 2) * 2
                                co = (ki % 2) * 512
                                ins = v.scalar_tensor_tensor(
                                    sscr[:, p % 2, co:co + 512],
                                    abs_sb[:, kt, :], slA,
                                    ps_sc[:, sl, :], op0=ALU.mult,
                                    op1=ALU.add)
                                if lhp == 1 and ki == half1_first:
                                    ins.then_inc(a_cons.h, 1)
                                ins = v.scalar_tensor_tensor(
                                    sscr[:, 2 + p % 2, co:co + 512],
                                    abs_sb[:, kt, :], slB,
                                    ps_sc[:, sl + 1, :], op0=ALU.mult,
                                    op1=ALU.add)
                                ins.then_inc(s_al.h, 1)
                        v.wait_ge(s_av.h, bases[1][3] + len(kts2[1]))
                        ins = v.tensor_copy(zrow[:, 0, :], ps_av[64:65, 0, :])
                        ins.then_inc(a_cons.h, 1)
                        for i in range(1, 4):
                            v.tensor_copy(zrow[:, i, :], ps_av[64:65, i, :])
                        ins = None
                        for i in range(4):
                            ins = v.reciprocal(rz[:, i, :], zrow[:, i, :])
                        ins.then_inc(a_rz.h, 1)
                        v.wait_ge(a_bz.h, bz0 + 1)
                        for i in range(4):
                            v.tensor_copy(bzs[:, i, :], ps_sc[0:64, i, :])
                        for lhp in range(2):
                            for side in range(2):
                                i = 2 * lhp + side
                                dst = attn_oT[side * 64:side * 64 + 64,
                                              2 * g + lhp,
                                              qb * 512:(qb + 1) * 512]
                                v.tensor_tensor(dst, ps_av[0:64, i, :],
                                                bzs[:, i, :], op=ALU.mult)

                    @blk.scalar
                    def _(s2, g=g, bases=bases, kts2=kts2):
                        for lhp in range(2):
                            mm0, al0, ex0, av0 = bases[lhp]
                            hA = 2 * (2 * g + lhp)
                            hB = hA + 1
                            for p in range(len(kts2[lhp]) // 2):
                                s2.wait_ge(s_al.h, al0 + 2 * p + 2)
                                if p >= 2:
                                    s2.wait_ge(s_av.h, av0 + 2 * (p - 1))
                                s2.activation(attnU[:, p % 2, :],
                                              sscr[:, p % 2, :],
                                              AF.Exp, scale=0.125,
                                              bias=cbias[:, hA:hA + 1])
                                ins = s2.activation(attnU[:, 2 + p % 2, :],
                                                    sscr[:, 2 + p % 2, :],
                                                    AF.Exp, scale=0.125,
                                                    bias=cbias[:, hB:hB + 1])
                                ins.then_inc(s_ex.h, 1)

                    if feeder is not None:
                        @blk.gpsimd
                        def _(gp):
                            feeder.feed(gp, 2)

def _eg_tiles():
    t = np.arange(32)
    return np.concatenate([t[0::2], t[1::2]])


def _rope_tables(positions):
    j = np.arange(64)
    inv = 1.0 / (10000.0 ** (2 * (j // 2) / DK))
    ang = positions[None, :] * inv[:, None]
    cos = np.cos(ang)
    sgn = np.where(j % 2 == 0, -1.0, 1.0)
    sinm = np.sin(ang) * sgn[:, None]
    cos2 = np.concatenate([cos, cos], 0)
    sin2 = np.concatenate([sinm, sinm], 0)
    return cos2.astype(np.float16), sin2.astype(np.float16)


def stage_l0(inputs, core):
    b, p = core // 2, core % 2
    x = np.asarray(inputs["L0_attn_x"][b], dtype=np.float32)
    xt = x.reshape(32, 128, D)
    own_tiles = np.arange(p, 32, 2)
    eg_tiles = _eg_tiles()
    pos_own = (own_tiles[:, None] * 128 + np.arange(128)[None, :]).reshape(-1)
    pos_eg = (eg_tiles[:, None] * 128 + np.arange(128)[None, :]).reshape(-1)
    cosq, sinq = _rope_tables(pos_own.astype(np.float64))
    cosk, sink = _rope_tables(pos_eg.astype(np.float64))
    return {
        "x_own": np.ascontiguousarray(xt[own_tiles].reshape(SH, D)),
        "x_eg": np.ascontiguousarray(xt[eg_tiles].reshape(S, D)),
        "wq": np.asarray(inputs["L0_attn_Wq"], np.float32),
        "wk": np.asarray(inputs["L0_attn_Wk"], np.float32),
        "wv": np.asarray(inputs["L0_attn_Wv"], np.float32),
        "wo": np.asarray(inputs["L0_attn_Wo"], np.float32),
        "wgate": np.asarray(inputs["L0_ffn_w_gate"], np.float32),
        "win_own": np.asarray(inputs["L0_ffn_w_in"][core], np.float32),
        "wout_own": np.asarray(inputs["L0_ffn_w_out"][core], np.float32),
        "cosq": cosq, "sinq": sinq, "cosk": cosk, "sink": sink,
        "posq": pos_own.astype(np.float32)[None, :],
        "posk": np.ascontiguousarray(
            pos_eg.astype(np.float32).reshape(32, 128).T),
        "identf": np.eye(128, dtype=np.float32),
    }


def unstage_x1(results, key="x1_own"):
    out = np.zeros((4, S, D), np.float32)
    for c in range(8):
        b, p = c // 2, c % 2
        own_tiles = np.arange(p, 32, 2)
        out[b].reshape(32, 128, D)[own_tiles] = \
            np.asarray(results[c][key]).reshape(16, 128, D)
    return out


def kernel_2launch(**inputs):
    nc0 = build_l0()
    maps = [stage_l0(inputs, c) for c in range(8)]
    res0 = run_bass_kernel_spmd(nc0, maps, core_ids=list(range(8)))
    x1 = unstage_x1(res0.results)
    nc1 = build_l1()
    maps1 = [stage_l1(inputs, x1, c) for c in range(8)]
    res1 = run_bass_kernel_spmd(nc1, maps1, core_ids=list(range(8)))
    return unstage_x1(res1.results, key="xout_own")


def build_l1(debug=()):
    nc = bass.Bass("TRN2", target_bir_lowering=False)
    io, scr = {}, {}

    def ein(nm, shape, dt=F32):
        io[nm] = nc.dram_tensor(nm, shape, dt, kind="ExternalInput").ap()

    def eout(nm, shape, dt=F32):
        io[nm] = nc.dram_tensor(nm, shape, dt, kind="ExternalOutput").ap()

    def dscr(nm, shape, dt=BF16):
        scr[nm] = nc.dram_tensor(nm, shape, dt)

    ein("x1_own", [SH, D]); ein("x1_eg", [S, D])
    ein("wgA", [D, D]); ein("wvA", [D, D]); ein("wdA", [D, D])
    ein("bgc", [128, 8]); ein("bvc", [128, 8]); ein("bdc", [128, 8])
    ein("l1s", [128, 8]); ein("l1h", [128, 8])
    ein("l2s", [128, 8]); ein("l2h", [128, 8])
    ein("fwq", [D, D]); ein("fwk", [D, D]); ein("fwv", [D, D]); ein("fwo", [D, D])
    ein("boc", [128, 8])
    ein("gwf", [D, E])
    ein("w1_own", [D, HID1]); ein("w2_own", [D, HID1]); ein("w3_own", [HID1, D])
    ein("mask_a", [128, 128]); ein("mask_b", [128, 128])
    ein("flag", [128, 2])
    ein("identf", [128, 128])
    eout("xout_own", [SH, D])

    dscr("y_own_bf", [SH, D]); dscr("y_eg_bf", [S, D])
    dscr("y_own_f32", [SH, D], F32)
    dscr("n1_own_bf", [SH, D]); dscr("n1_eg_bf", [S, D])
    dscr("wgA_bf", [D, D]); dscr("wvA_bf", [D, D]); dscr("wdA_bf", [D, D])
    dscr("fwq_bf", [D, D]); dscr("fwk_bf", [D, D]); dscr("fwv_bf", [D, D])
    dscr("fwo_bf", [D, D])
    dscr("w1_bf", [D, HID1]); dscr("w2_bf", [D, HID1]); dscr("w3_bf", [HID1, D])
    dscr("w1_all", [E * D, HID1]); dscr("w2_all", [E * D, HID1])
    dscr("w3_all", [E * HID1, D])
    dscr("A_dram", [128, 8, S], F32); dscr("B_dram", [128, 8, S], F32)
    dscr("s_dram", [128, 8, SH], F32)
    dscr("kT_dram", [128, 8, S]); dscr("vS_dram", [128, 32, H * 65])
    dscr("y2T_f32", [128, 8, SH], F32)
    dscr("n2_own_bf", [SH, D]); dscr("n2_own_lo", [SH, D])
    dscr("grow2", [E, SH], F32)
    dscr("g2_dram", [D, SH])

    dbg = {}
    for nm, shape, dt in [
        ("s_rnn", [128, 8, SH], F32), ("aoT2", [128, 8, SH], BF16),
        ("y2T", [128, 8, SH], F32), ("gates2", [128, 16, E], F32),
        ("g2", [D, SH], BF16), ("cb2", [128, 16], F32),
        ("n2", [SH, D], BF16),
    ]:
        if nm in debug:
            dbg[nm] = nc.dram_tensor("dbg_" + nm, shape, dt,
                                     kind="ExternalOutput").ap()

    scr = {k: v.ap() for k, v in scr.items()}
    with contextlib.ExitStack() as ctx:
        em = Emitter(nc, ctx)
        _emit_l1(em, io, scr, dbg)
    return nc


def _l1_weight_prep(em, io, scr):
    nc = em.nc
    with contextlib.ExitStack() as pc:
        wcast = pc.enter_context(nc.sbuf_tensor("wcast1", [128, 4, HID1], BF16))
        for srcn, dstn, rows, cols in [
            ("wgA", "wgA_bf", D, D), ("wvA", "wvA_bf", D, D),
            ("wdA", "wdA_bf", D, D), ("fwq", "fwq_bf", D, D),
            ("fwk", "fwk_bf", D, D), ("fwv", "fwv_bf", D, D),
            ("fwo", "fwo_bf", D, D), ("w1_own", "w1_bf", D, HID1),
            ("w2_own", "w2_bf", D, HID1), ("w3_own", "w3_bf", HID1, D),
        ]:
            em.cast_to_bf16_dram(io[srcn], scr[dstn], rows, cols, wcast)
    for own, alln in (("w1_bf", "w1_all"), ("w2_bf", "w2_all"),
                      ("w3_bf", "w3_all")):
        with nc.Block() as blk:
            @blk.gpsimd
            def _(gp, own=own, alln=alln):
                gp.collective_compute(
                    "AllGather", ALU.bypass,
                    replica_groups=[list(range(8))],
                    ins=[scr[own].opt()], outs=[scr[alln].opt()],
                ).then_inc(em.cc.h)
        em.cc.v += 1
    with nc.Block() as blk:
        @blk.gpsimd
        def _(gp):
            gp.wait_ge(em.cc.h, em.cc.v)


def _emit_l1(em, io, scr, dbg, fused=False):
    nc, ctx, sb = em.nc, em.ctx, em.sb

    cbias = sb("cbias1", [128, 16], F32)
    identf_sb = sb("identf_sb1", [128, 128], F32)
    ones_row = sb("ones_row1", [1, 128], BF16)
    onesf_row = sb("onesf_row1", [1, 128], F32)
    onescol = sb("onescol1", [128, 1], BF16)
    epscol = sb("epscol1", [128, 1], F32)
    bcols = sb("bcols", [128, 8, 8], F32)
    flag_sb = sb("flag_sb", [128, 2], F32)
    mask_a = sb("mask_a_sb", [128, 128], F32)
    mask_b = sb("mask_b_sb", [128, 128], F32)

    em.load([(identf_sb[:], io["identf"]), (flag_sb[:], io["flag"]),
             (mask_a[:], io["mask_a"]), (mask_b[:], io["mask_b"])] +
            [(bcols[:, i, :], io[nm]) for i, nm in enumerate(
                ("bgc", "bvc", "bdc", "l1s", "l1h", "l2s", "l2h", "boc"))])
    BGC, BVC, BDC, L1S, L1H, L2S, L2H, BOC = range(8)
    with nc.Block() as blk:
        @blk.gpsimd
        def _(gp):
            gp.memset(ones_row[:], 1.0)
            gp.memset(onesf_row[:], 1.0)
            gp.memset(onescol[:], 1.0)
            gp.memset(epscol[:], EPS)

    if not fused:
        _l1_weight_prep(em, io, scr)

    with contextlib.ExitStack() as pc:
        sbp = lambda nm, shp, dt: pc.enter_context(nc.sbuf_tensor(nm, shp, dt))
        xld = sbp("c_xld", [128, 8, D], F32)
        yf = sbp("c_yf", [128, 8, D], F32)
        n1f = sbp("c_n1f", [128, 8, D], F32)
        sq = sbp("c_sq", [128, 2, D], F32)
        st = sbp("c_st", [128, 8, 8], F32)
        st2 = sbp("c_st2", [128, 8, 8], F32)

        tiles = []
        if not fused:
            for t in range(32):
                tiles.append(("eg", t))
        for t in range(16):
            tiles.append(("own", t))
        x1_src = scr["x1_dram"] if fused else None

        for g0 in range(0, 48, 8):
            grp = tiles[g0:g0 + 8]
            m = len(grp)
            with nc.Block() as blk:
                @blk.sync
                def _(s, grp=grp):
                    for i, (kind, t) in enumerate(grp):
                        if fused:
                            src = x1_src[t * 128:(t + 1) * 128, :]
                        else:
                            src = (io["x1_eg"] if kind == "eg" else
                                   io["x1_own"])[t * 128:(t + 1) * 128, :]
                        em.ld.inc(s.dma_start(out=xld[:, i, :], in_=src))

                @blk.scalar
                def _(s2, m=m):
                    s2.wait_ge(em.ld.h, em.ld.v)
                    for i in range(m):
                        s2.activation(sq[:, i % 2, :], xld[:, i, :], AF.Square,
                                      accum_out=st[:, i, 0:1])

                @blk.vector
                def _(v, m=m):
                    v.wait_ge(em.ld.h, em.ld.v)
                    for i in range(m):
                        v.reduce_sum(st[:, i, 1:2], xld[:, i, :], axis=AX.X)

            def stats_blocks(stx):
                with nc.Block() as blk:
                    @blk.vector
                    def _(v, stx=stx, m=m):
                        for i in range(m):
                            v.tensor_scalar_mul(stx[:, i, 2:3], stx[:, i, 1:2],
                                                1.0 / D)
                        for i in range(m):
                            v.tensor_scalar_mul(stx[:, i, 3:4], stx[:, i, 0:1],
                                                1.0 / D)
                        for i in range(m):
                            v.tensor_tensor(stx[:, i, 4:5], stx[:, i, 2:3],
                                            stx[:, i, 2:3], op=ALU.mult)
                        for i in range(m):
                            v.tensor_tensor(stx[:, i, 3:4], stx[:, i, 3:4],
                                            stx[:, i, 4:5], op=ALU.subtract)
                with nc.Block() as blk:
                    @blk.scalar
                    def _(s2, stx=stx, m=m):
                        for i in range(m):
                            s2.activation(stx[:, i, 5:6], stx[:, i, 3:4],
                                          AF.Sqrt, bias=epscol[:])
                with nc.Block() as blk:
                    @blk.vector
                    def _(v, stx=stx, m=m):
                        for i in range(m):
                            v.reciprocal(stx[:, i, 6:7], stx[:, i, 5:6])

            stats_blocks(st)
            with nc.Block() as blk:
                @blk.vector
                def _(v, m=m):
                    for i in range(m):
                        v.tensor_scalar(yf[:, i, :], xld[:, i, :], st[:, i, 2:3],
                                        st[:, i, 6:7], op0=ALU.subtract,
                                        op1=ALU.mult)
            with nc.Block() as blk:
                @blk.scalar
                def _(s2, m=m):
                    for i in range(m):
                        s2.activation(sq[:, i % 2, :], yf[:, i, :], AF.Square,
                                      accum_out=st2[:, i, 0:1])

                @blk.vector
                def _(v, m=m):
                    for i in range(m):
                        v.reduce_sum(st2[:, i, 1:2], yf[:, i, :], axis=AX.X)
            stats_blocks(st2)
            with nc.Block() as blk:
                @blk.vector
                def _(v, m=m):
                    for i in range(m):
                        v.tensor_scalar(n1f[:, i, :], yf[:, i, :],
                                        st2[:, i, 2:3], st2[:, i, 6:7],
                                        op0=ALU.subtract, op1=ALU.mult)
            with nc.Block() as blk:
                @blk.sync
                def _(s, grp=grp):
                    for i, (kind, t) in enumerate(grp):
                        if kind == "own":
                            em.so.inc(s.dma_start(
                                out=scr["y_own_f32"][t * 128:(t + 1) * 128, :],
                                in_=yf[:, i, :]))

                @blk.gpsimd
                def _(gp, grp=grp):
                    for i, (kind, t) in enumerate(grp):
                        ydst = (scr["y_eg_bf"] if kind == "eg" else
                                scr["y_own_bf"])[t * 128:(t + 1) * 128, :]
                        em.gso.inc(gp.dma_start(out=ydst, in_=yf[:, i, :]))
                        ndst = (scr["n1_eg_bf"] if kind == "eg" else
                                scr["n1_own_bf"])[t * 128:(t + 1) * 128, :]
                        em.gso.inc(gp.dma_start(out=ndst, in_=n1f[:, i, :]))
                    gp.wait_ge(em.gso.h, em.gso.v)
    if fused:
        for own, eg in (("y_own_bf", "y_eg_bf"), ("n1_own_bf", "n1_eg_bf")):
            with nc.Block() as blk:
                @blk.gpsimd
                def _(gp, own=own, eg=eg):
                    gp.collective_compute(
                        "AllGather", ALU.bypass,
                        replica_groups=[[0, 1], [2, 3], [4, 5], [6, 7]],
                        ins=[scr[own].opt()], outs=[scr[eg].opt()],
                    ).then_inc(em.cc.h)
            em.cc.v += 1
        with nc.Block() as blk:
            @blk.gpsimd
            def _(gp):
                gp.wait_ge(em.cc.h, em.cc.v)

    _l1_rnn(em, io, scr, dbg, flag_sb, bcols, BGC, BVC, BDC)
    _l1_attn(em, io, scr, dbg, cbias, bcols, L1S, L1H, BOC, onescol, ones_row,
             onesf_row, identf_sb, mask_a, mask_b, epscol)
    _l1_moe(em, io, scr, dbg, bcols, L2S, L2H, ones_row, onesf_row, identf_sb)
    _l1_final(em, io, scr, dbg, identf_sb)


def _l1_rnn(em, io, scr, dbg, flag_sb, bcols, BGC, BVC, BDC):
    nc = em.nc
    with contextlib.ExitStack() as pc:
        sbp = lambda nm, shp, dt: pc.enter_context(nc.sbuf_tensor(nm, shp, dt))
        psb = lambda nm, shp: pc.enter_context(nc.psum_tensor(nm, shp, F32))
        g1T = sbp("g1T", [128, 8, SH], BF16)
        ends_A = sbp("ends_A", [128, 8, 32], F32)
        ends_B = sbp("ends_B", [128, 8, 32], F32)
        states = sbp("states", [128, 8, 33], F32)
        zeros = sbp("zeros", [128, 512], F32)
        with nc.Block() as blk:
            @blk.gpsimd
            def _(gp):
                gp.memset(zeros[:], 0.0)
                gp.memset(states[:], 0.0)

        with contextlib.ExitStack() as p2:
            yT_own = p2.enter_context(nc.sbuf_tensor("yT_own", [128, 8, SH],
                                                     BF16))
            wT = p2.enter_context(nc.sbuf_tensor("rwT", [128, 8, D], BF16))
            ps = p2.enter_context(nc.psum_tensor("rps", [128, 4, 512], F32))
            em.load_wT(yT_own, scr["y_own_bf"], rows=SH)
            em.load_wT(wT, scr["wgA_bf"])
            g_mm = Sem(em.ctx, nc, "g_mm")
            g_sg = Sem(em.ctx, nc, "g_sg")
            with nc.Block() as blk:
                @blk.tensor
                def _(t):
                    for dt_ in range(8):
                        for tb in range(4):
                            k = dt_ * 4 + tb
                            if k >= 4:
                                t.wait_ge(g_sg.h, k - 3)
                            for dc in range(8):
                                ins = t.matmul(
                                    ps[:, tb, :],
                                    wT[:, dc, dt_ * 128:(dt_ + 1) * 128],
                                    yT_own[:, dc, tb * 512:(tb + 1) * 512],
                                    start=(dc == 0), stop=(dc == 7))
                            ins.then_inc(g_mm.h, 1)

                @blk.scalar
                def _(s2):
                    for dt_ in range(8):
                        for tb in range(4):
                            k = dt_ * 4 + tb
                            s2.wait_ge(g_mm.h, k + 1)
                            ins = s2.activation(
                                g1T[:, dt_, tb * 512:(tb + 1) * 512],
                                ps[:, tb, :], AF.Sigmoid,
                                bias=bcols[:, BGC, dt_:dt_ + 1])
                            ins.then_inc(g_sg.h, 1)
            g_mm.v += 32
            g_sg.v += 32

        with contextlib.ExitStack() as p2:
            yT_eg = p2.enter_context(nc.sbuf_tensor("yT_eg", [128, 8, S], BF16))
            wTd = p2.enter_context(nc.sbuf_tensor("wTd", [128, 8, D], BF16))
            wTv = p2.enter_context(nc.sbuf_tensor("wTv", [128, 8, D], BF16))
            psa = p2.enter_context(nc.psum_tensor("psa", [128, 4, 512], F32))
            psv = p2.enter_context(nc.psum_tensor("psv", [128, 4, 512], F32))
            ach = p2.enter_context(nc.sbuf_tensor("ach", [128, 4, 512], F32))
            vch = p2.enter_context(nc.sbuf_tensor("vch", [128, 4, 512], F32))
            wch = p2.enter_context(nc.sbuf_tensor("wch", [128, 4, 512], F32))
            Ach = p2.enter_context(nc.sbuf_tensor("Ach", [128, 4, 512], F32))
            Bch = p2.enter_context(nc.sbuf_tensor("Bch", [128, 4, 512], F32))
            em.load_wT(yT_eg, scr["y_eg_bf"], rows=S)
            em.load_wT(wTd, scr["wdA_bf"])
            em.load_wT(wTv, scr["wvA_bf"])
            rt_mm = Sem(em.ctx, nc, "rt_mm")
            rv_a = Sem(em.ctx, nc, "rv_a")
            rv_w = Sem(em.ctx, nc, "rv_w")
            rv_sc = Sem(em.ctx, nc, "rv_sc")
            rv_st = Sem(em.ctx, nc, "rv_st")
            for c in range(8):
                for half in range(2):
                    mm0, a0, w0, sc0, st0 = (rt_mm.v, rv_a.v, rv_w.v,
                                             rv_sc.v, rv_st.v)
                    with nc.Block() as blk:
                        @blk.tensor
                        def _(t, c=c, half=half):
                            for i in range(4):
                                dt_ = half * 4 + i
                                for dc in range(8):
                                    t.matmul(
                                        psa[:, i, :],
                                        wTd[:, dc, dt_ * 128:(dt_ + 1) * 128],
                                        yT_eg[:, dc, c * 512:(c + 1) * 512],
                                        start=(dc == 0), stop=(dc == 7))
                                for dc in range(8):
                                    ins = t.matmul(
                                        psv[:, i, :],
                                        wTv[:, dc, dt_ * 128:(dt_ + 1) * 128],
                                        yT_eg[:, dc, c * 512:(c + 1) * 512],
                                        start=(dc == 0), stop=(dc == 7))
                                ins.then_inc(rt_mm.h, 1)

                        @blk.scalar
                        def _(s2, c=c, half=half):
                            for i in range(4):
                                dt_ = half * 4 + i
                                s2.wait_ge(rt_mm.h, mm0 + i + 1)
                                ins = s2.activation(ach[:, i, :], psa[:, i, :],
                                                    AF.Sigmoid,
                                                    bias=bcols[:, BDC,
                                                               dt_:dt_ + 1])
                                ins.then_inc(rv_a.h, 1)

                        @blk.vector
                        def _(v, c=c, half=half):
                            for i in range(4):
                                dt_ = half * 4 + i
                                v.wait_ge(rt_mm.h, mm0 + i + 1)
                                v.tensor_scalar(vch[:, i, :], psv[:, i, :],
                                                bcols[:, BVC, dt_:dt_ + 1],
                                                None, op0=ALU.add)
                                v.wait_ge(rv_a.h, a0 + i + 1)
                                v.tensor_scalar(wch[:, i, :], ach[:, i, :],
                                                -1.0, 1.0, op0=ALU.mult,
                                                op1=ALU.add)
                                ins = v.tensor_tensor(wch[:, i, :],
                                                      wch[:, i, :],
                                                      vch[:, i, :],
                                                      op=ALU.mult)
                                ins.then_inc(rv_w.h, 1)
                            for i in range(4):
                                for ti in range(4):
                                    o = ti * 128
                                    v.tensor_tensor_scan(
                                        Ach[:, i, o:o + 128],
                                        ach[:, i, o:o + 128], zeros[:, 0:128],
                                        1.0, op0=ALU.mult, op1=ALU.add)
                                for ti in range(4):
                                    o = ti * 128
                                    v.tensor_tensor_scan(
                                        Bch[:, i, o:o + 128],
                                        ach[:, i, o:o + 128],
                                        wch[:, i, o:o + 128],
                                        0.0, op0=ALU.mult, op1=ALU.add)
                                ins = v.engine_nop()
                                ins.then_inc(rv_sc.h, 1)
                            for i in range(4):
                                dt_ = half * 4 + i
                                for ti in range(4):
                                    lt = c * 4 + ti
                                    o = ti * 128 + 127
                                    v.tensor_copy(ends_A[:, dt_, lt:lt + 1],
                                                  Ach[:, i, o:o + 1])
                                    ins = v.tensor_copy(
                                        ends_B[:, dt_, lt:lt + 1],
                                        Bch[:, i, o:o + 1])
                                ins.then_inc(rv_st.h, 1)

                        @blk.sync
                        def _(s, c=c, half=half):
                            s.wait_ge(rv_st.h, st0 + 4)
                            em.so.inc(s.dma_start(
                                out=scr["A_dram"][:, half * 4:(half + 1) * 4,
                                                  c * 512:(c + 1) * 512],
                                in_=Ach[:]))
                            em.so.inc(s.dma_start(
                                out=scr["B_dram"][:, half * 4:(half + 1) * 4,
                                                  c * 512:(c + 1) * 512],
                                in_=Bch[:]))
                            s.wait_ge(em.so.h, em.so.v)
                    rt_mm.v += 4
                    rv_a.v += 4
                    rv_w.v += 4
                    rv_sc.v += 4
                    rv_st.v += 4

        with nc.Block() as blk:
            @blk.vector
            def _(v):
                for m in range(32):
                    lt = (m // 2) if m % 2 == 0 else 16 + m // 2
                    for dt_ in range(8):
                        v.tensor_scalar(states[:, dt_, m + 1:m + 2],
                                        ends_A[:, dt_, lt:lt + 1],
                                        states[:, dt_, m:m + 1],
                                        ends_B[:, dt_, lt:lt + 1],
                                        op0=ALU.mult, op1=ALU.add)

        with contextlib.ExitStack() as p2:
            At = p2.enter_context(nc.sbuf_tensor("At", [128, 2, 8, 2, 128],
                                                 F32))
            Bt = p2.enter_context(nc.sbuf_tensor("Bt", [128, 2, 8, 2, 128],
                                                 F32))
            he = p2.enter_context(nc.sbuf_tensor("he", [128, 8, 128], F32))
            ho = p2.enter_context(nc.sbuf_tensor("ho", [128, 8, 128], F32))
            hb = p2.enter_context(nc.sbuf_tensor("hb", [128, 8, 128], F32))
            srt = p2.enter_context(nc.sbuf_tensor("srt", [128, 2, 8, 128],
                                                  F32))
            p_ld = Sem(em.ctx, nc, "p_ld")
            p_bl = Sem(em.ctx, nc, "p_bl")
            ld_mk = {}
            st_mk = {}
            p_bl.v += 16
            with nc.Block() as blk:
                @blk.sync
                def _(s):
                    def store_j(j):
                        em.so.inc(s.dma_start(
                            out=scr["s_dram"][:, :, j * 128:(j + 1) * 128],
                            in_=srt[:, j % 2, :, :]))
                        st_mk[j] = em.so.v

                    for j in range(16):
                        if j >= 2:
                            s.wait_ge(p_bl.h, j - 1)
                        for dst, srcn, t0 in (
                                (At[:, j % 2, :, 0, :], "A_dram", j),
                                (Bt[:, j % 2, :, 0, :], "B_dram", j),
                                (At[:, j % 2, :, 1, :], "A_dram", 16 + j),
                                (Bt[:, j % 2, :, 1, :], "B_dram", 16 + j)):
                            p_ld.inc(s.dma_start(
                                out=dst,
                                in_=scr[srcn][:, :, t0 * 128:(t0 + 1) * 128]))
                        ld_mk[j] = p_ld.v
                        if j >= 1:
                            s.wait_ge(p_bl.h, j)
                            store_j(j - 1)
                    s.wait_ge(p_bl.h, 16)
                    store_j(15)
                    s.wait_ge(em.so.h, em.so.v)

                @blk.vector
                def _(v):
                    for j in range(16):
                        v.wait_ge(p_ld.h, ld_mk[j])
                        if j >= 2:
                            v.wait_ge(em.so.h, st_mk[j - 2])
                        for dt_ in range(8):
                            v.scalar_tensor_tensor(
                                he[:, dt_, :], At[:, j % 2, dt_, 0, :],
                                states[:, dt_, 2 * j:2 * j + 1],
                                Bt[:, j % 2, dt_, 0, :], op0=ALU.mult,
                                op1=ALU.add)
                        for dt_ in range(8):
                            v.scalar_tensor_tensor(
                                ho[:, dt_, :], At[:, j % 2, dt_, 1, :],
                                states[:, dt_, 2 * j + 1:2 * j + 2],
                                Bt[:, j % 2, dt_, 1, :], op0=ALU.mult,
                                op1=ALU.add)
                        for dt_ in range(8):
                            v.tensor_scalar(hb[:, dt_, :], he[:, dt_, :],
                                            flag_sb[:, 0:1], None,
                                            op0=ALU.mult)
                        for dt_ in range(8):
                            v.scalar_tensor_tensor(
                                hb[:, dt_, :], ho[:, dt_, :],
                                flag_sb[:, 1:2], hb[:, dt_, :],
                                op0=ALU.mult, op1=ALU.add)
                        ins = None
                        for dt_ in range(8):
                            ins = v.tensor_tensor(
                                srt[:, j % 2, dt_, :], hb[:, dt_, :],
                                g1T[:, dt_, j * 128:(j + 1) * 128],
                                op=ALU.mult)
                        ins.then_inc(p_bl.h, 1)
    if "s_rnn" in dbg:
        em.store([(dbg["s_rnn"], scr["s_dram"])])


def _l1_attn(em, io, scr, dbg, cbias, bcols, L1S, L1H, BOC, onescol, ones_row,
             onesf_row, identf_sb, mask_a, mask_b, epscol):
    nc = em.nc
    with contextlib.ExitStack() as scA:
        qT = scA.enter_context(nc.sbuf_tensor("qT1", [128, 8, SH], BF16))
        attn_oT = scA.enter_context(nc.sbuf_tensor("aoT1", [128, 8, SH], BF16))
        bqk = scA.enter_context(nc.sbuf_tensor("bqk", [128, 2, 8], F32))
        bvw = scA.enter_context(nc.sbuf_tensor("bvw", [1, D], BF16))

        with contextlib.ExitStack() as pc:
            sbp = lambda nm, shp, dt: pc.enter_context(
                nc.sbuf_tensor(nm, shp, dt))
            psb = lambda nm, shp: pc.enter_context(nc.psum_tensor(nm, shp, F32))
            n1T_own = sbp("n1T_own", [128, 8, SH], BF16)
            n1T_eg = sbp("n1T_eg", [128, 8, S], BF16)
            wnat = sbp("wnat", [128, 8, D], BF16)
            bchunk = sbp("bchunk", [128, 8], BF16)
            ps = psb("aps", [128, 4, 512])
            ps_n = psb("aps_n", [128, 4, 512])
            nsq = sbp("ansq", [128, 2, 512], BF16)
            kT_g = sbp("akTg", [128, 2, S], BF16)
            qmx = sbp("aqmx", [1, H, 4], F32)
            kmx = sbp("akmx", [1, H, 8], F32)
            cbs = sbp("acbs", [1, 6 * H], F32)
            vS_tile = sbp("avS", [128, 2, H * 65], BF16)

            em.load_wT(n1T_own, scr["n1_own_bf"], rows=SH)
            em.load_wT(n1T_eg, scr["n1_eg_bf"], rows=S)
            with nc.Block() as blk:
                @blk.vector
                def _(v):
                    for dc in range(8):
                        v.tensor_scalar(n1T_own[:, dc, :], n1T_own[:, dc, :],
                                        bcols[:, L1S, dc:dc + 1], None,
                                        op0=ALU.mult)
                    for dc in range(8):
                        v.tensor_scalar(n1T_eg[:, dc, :], n1T_eg[:, dc, :],
                                        bcols[:, L1S, dc:dc + 1], None,
                                        op0=ALU.mult)
                    v.tensor_copy(bchunk[:], bcols[:, L1H, :])

            def load_nat(wsrc):
                with nc.Block() as blk:
                    @blk.gpsimd
                    def _(gp, wsrc=wsrc):
                        for dc in range(8):
                            em.gld.inc(gp.dma_start(
                                out=wnat[:, dc, :],
                                in_=wsrc[dc * 128:(dc + 1) * 128, :]))
                        gp.wait_ge(em.gld.h, em.gld.v)

            def beta_cols(wi):
                with nc.Block() as blk:
                    @blk.tensor
                    def _(t):
                        for oc in range(8):
                            for dc in range(8):
                                t.matmul(ps[:, 0, oc:oc + 1],
                                         wnat[:, dc, oc * 128:(oc + 1) * 128],
                                         bchunk[:, dc:dc + 1],
                                         start=(dc == 0), stop=(dc == 7))
                with nc.Block() as blk:
                    @blk.vector
                    def _(v, wi=wi):
                        v.tensor_copy(bqk[:, wi, :], ps[:, 0, 0:8])

            f_mm = Sem(em.ctx, nc, "f_mm")
            f_ev = Sem(em.ctx, nc, "f_ev")
            f_sq = Sem(em.ctx, nc, "f_sq")
            f_nm = Sem(em.ctx, nc, "f_nm")
            f_mx = Sem(em.ctx, nc, "f_mx")

            def proj_pipe(srcT, hp, nb, out_fn, bsl, mx):
                b = dict(mm=f_mm.v, ev=f_ev.v, sq=f_sq.v, nm=f_nm.v,
                         mx=f_mx.v)
                f_mm.v += nb
                f_ev.v += nb
                f_sq.v += nb
                f_nm.v += nb
                f_mx.v += nb
                with nc.Block() as blk:
                    @blk.tensor
                    def _(t, hp=hp, b=b, nb=nb, srcT=srcT):
                        def norm_mm(j):
                            t.wait_ge(f_sq.h, b["sq"] + j + 1)
                            if j >= 2:
                                t.wait_ge(f_mx.h, b["mx"] + j - 1)
                            sl = 2 * (j % 2)
                            t.matmul(ps_n[0:1, sl, :], onescol[0:64, :],
                                     nsq[0:64, j % 2, :], start=True,
                                     stop=True)
                            ins = t.matmul(ps_n[0:1, sl + 1, :],
                                           onescol[64:128, :],
                                           nsq[64:128, j % 2, :], start=True,
                                           stop=True, tile_position=(64, 0))
                            ins.then_inc(f_nm.h, 1)

                        for i in range(nb):
                            if i >= 2:
                                t.wait_ge(f_ev.h, b["ev"] + i - 1)
                            for dc in range(8):
                                ins = t.matmul(
                                    ps[:, i % 2, :],
                                    wnat[:, dc, hp * 128:(hp + 1) * 128],
                                    srcT[:, dc, i * 512:(i + 1) * 512],
                                    start=(dc == 0), stop=(dc == 7))
                            ins.then_inc(f_mm.h, 1)
                            if i >= 1:
                                norm_mm(i - 1)
                        norm_mm(nb - 1)

                    @blk.vector
                    def _(v, hp=hp, b=b, nb=nb, mx=mx, out_fn=out_fn,
                          bsl=bsl):
                        def vmax(j):
                            v.wait_ge(f_nm.h, b["nm"] + j + 1)
                            sl = 2 * (j % 2)
                            v.reduce_max(mx[:, 2 * hp, j:j + 1],
                                         ps_n[0:1, sl, :], axis=AX.X)
                            ins = v.reduce_max(mx[:, 2 * hp + 1, j:j + 1],
                                               ps_n[0:1, sl + 1, :],
                                               axis=AX.X)
                            ins.then_inc(f_mx.h, 1)

                        for i in range(nb):
                            v.wait_ge(f_mm.h, b["mm"] + i + 1)
                            ins = v.tensor_scalar(out_fn(i), ps[:, i % 2, :],
                                                  bsl, None, op0=ALU.add)
                            ins.then_inc(f_ev.h, 1)
                            if i >= 1:
                                vmax(i - 1)
                        vmax(nb - 1)

                    @blk.scalar
                    def _(s2, b=b, nb=nb, out_fn=out_fn):
                        for i in range(nb):
                            s2.wait_ge(f_ev.h, b["ev"] + i + 1)
                            if i >= 2:
                                s2.wait_ge(f_nm.h, b["nm"] + i - 1)
                            ins = s2.activation(nsq[:, i % 2, :], out_fn(i),
                                                AF.Square)
                            ins.then_inc(f_sq.h, 1)

            load_nat(io["fwq"])
            beta_cols(0)
            for hp in range(8):
                proj_pipe(n1T_own, hp, 4,
                          lambda i, hp=hp: qT[:, hp, i * 512:(i + 1) * 512],
                          bqk[:, 0, hp:hp + 1], qmx)
            load_nat(io["fwk"])
            beta_cols(1)
            for hp in range(8):
                lhp = hp % 2
                proj_pipe(n1T_eg, hp, 8,
                          lambda i, lhp=lhp: kT_g[:, lhp,
                                                  i * 512:(i + 1) * 512],
                          bqk[:, 1, hp:hp + 1], kmx)
                if hp % 2 == 1:
                    em.store([(scr["kT_dram"][:, hp - 1, :], kT_g[:, 0, :]),
                              (scr["kT_dram"][:, hp, :], kT_g[:, 1, :])])

            load_nat(io["fwv"])
            with nc.Block() as blk:
                @blk.tensor
                def _(t):
                    for vb in range(2):
                        for dc in range(8):
                            t.matmul(ps_n[0:1, vb, :], bchunk[:, dc:dc + 1],
                                     wnat[:, dc, vb * 512:(vb + 1) * 512],
                                     start=(dc == 0), stop=(dc == 7))
            with nc.Block() as blk:
                @blk.vector
                def _(v):
                    for vb in range(2):
                        v.tensor_copy(bvw[0:1, vb * 512:(vb + 1) * 512],
                                      ps_n[0:1, vb, :])
            with nc.Block() as blk:
                @blk.gpsimd
                def _(gp):
                    for sl in range(2):
                        gp.memset(vS_tile[:, sl, :].rearrange(
                            "p (h c) -> p h c", c=65)[:, :, 64:65], 1.0)
            pv_mm = Sem(em.ctx, nc, "f_pvmm")
            pv_cp = Sem(em.ctx, nc, "f_pvcp")
            st_mk = {}
            pv_mm.v += 32
            pv_cp.v += 32
            with nc.Block() as blk:
                @blk.tensor
                def _(t):
                    for tt in range(32):
                        if tt >= 2:
                            t.wait_ge(pv_cp.h, tt - 1)
                        for vb in range(2):
                            sl = (tt % 2) * 2 + vb
                            for dc in range(8):
                                t.matmul(ps[:, sl, :],
                                         n1T_eg[:, dc,
                                                tt * 128:(tt + 1) * 128],
                                         wnat[:, dc, vb * 512:(vb + 1) * 512],
                                         start=(dc == 0), stop=False)
                            ins = t.matmul(ps[:, sl, :], ones_row[:],
                                           bvw[0:1, vb * 512:(vb + 1) * 512],
                                           start=False, stop=True)
                        ins.then_inc(pv_mm.h, 1)

                @blk.sync
                def _(s):
                    for tt in range(32):
                        s.wait_ge(pv_cp.h, tt + 1)
                        em.so.inc(s.dma_start(
                            out=scr["vS_dram"][:, tt, :],
                            in_=vS_tile[:, tt % 2, :]))
                        st_mk[tt] = em.so.v
                    s.wait_ge(em.so.h, em.so.v)

                @blk.vector
                def _(v):
                    for tt in range(32):
                        v.wait_ge(pv_mm.h, tt + 1)
                        if tt >= 2:
                            v.wait_ge(em.so.h, st_mk[tt - 2])
                        ins = None
                        for vb in range(2):
                            sl = (tt % 2) * 2 + vb
                            dst = vS_tile[:, tt % 2, :].rearrange(
                                "p (h c) -> p h c",
                                c=65)[:, vb * 8:(vb + 1) * 8, 0:64]
                            ins = v.tensor_copy(dst, ps[:, sl, :].rearrange(
                                "p (h c) -> p h c", c=64))
                        ins.then_inc(pv_cp.h, 1)

            with nc.Block() as blk:
                @blk.vector
                def _(v):
                    for h in range(H):
                        v.reduce_max(cbs[:, h:h + 1], qmx[:, h, :], axis=AX.X)
                    for h in range(H):
                        v.reduce_max(cbs[:, H + h:H + h + 1], kmx[:, h, :],
                                     axis=AX.X)
            with nc.Block() as blk:
                @blk.vector
                def _(v):
                    v.tensor_tensor(cbs[:, 2 * H:3 * H], cbs[:, 0:H],
                                    cbs[:, H:2 * H], op=ALU.mult)
            with nc.Block() as blk:
                @blk.scalar
                def _(s2):
                    s2.activation(cbs[:, 3 * H:4 * H], cbs[:, 2 * H:3 * H],
                                  AF.Sqrt)
            with nc.Block() as blk:
                @blk.tensor
                def _(t):
                    t.matmul(ps[:, 0, 0:16], onesf_row[:], cbs[:, 3 * H:4 * H],
                             start=True, stop=True)
            with nc.Block() as blk:
                @blk.vector
                def _(v):
                    v.tensor_scalar_mul(cbias[:], ps[:, 0, 0:16], -0.125)
        if "cb2" in dbg:
            em.store([(dbg["cb2"], cbias[:])])

        with contextlib.ExitStack() as pc:
            sbp = lambda nm, shp, dt: pc.enter_context(
                nc.sbuf_tensor(nm, shp, dt))
            kT_g = sbp("ckTg", [128, 2, S], BF16)
            v_g = sbp("cvg", [128, 32, 4 * 65], BF16)
            attnU = sbp("cattnU", [128, 4, 512], BF16)
            zrow = sbp("czrow", [1, 4, 128], F32)
            rz = sbp("crz", [1, 4, 128], F32)
            bzs = sbp("cbzs", [64, 4, 128], F32)
            ps_sc = pc.enter_context(nc.psum_tensor("cps_sc", [128, 4, 512],
                                                    F32))
            ps_av = pc.enter_context(nc.psum_tensor("cps_av", [128, 4, 512],
                                                    F32))
            s_mm = Sem(em.ctx, nc, "c_mm")
            s_al = Sem(em.ctx, nc, "c_al")
            s_ex = Sem(em.ctx, nc, "c_ex")
            s_av = Sem(em.ctx, nc, "c_av")
            c_rz = Sem(em.ctx, nc, "c_rz")
            c_bz = Sem(em.ctx, nc, "c_bz")

            for g in range(4):
                em.load([(kT_g[:, 0, :], scr["kT_dram"][:, 2 * g, :]),
                         (kT_g[:, 1, :], scr["kT_dram"][:, 2 * g + 1, :]),
                         (v_g[:], scr["vS_dram"][:, :, g * 260:(g + 1) * 260])])
                for j in range(16):
                    kts = [(i, None) for i in range(j)] + [(j, mask_a)] + \
                          [(16 + i, None) for i in range(j)] + \
                          [(16 + j, mask_b)]
                    chunks = [kts[i:i + 4] for i in range(0, len(kts), 4)]
                    NC_ = len(chunks)
                    bases = []
                    for lhp in range(2):
                        bases.append((s_mm.v, s_al.v, s_ex.v, s_av.v))
                        s_mm.v += NC_
                        s_al.v += NC_
                        s_ex.v += NC_
                        s_av.v += NC_
                    rz0, bz0 = c_rz.v, c_bz.v
                    c_rz.v += 1
                    c_bz.v += 1
                    with nc.Block() as blk:
                        def av_chunk(t, ci, lhp, chunks=chunks):
                            ch = chunks[ci]
                            ins = None
                            for pos, (kt, _m) in enumerate(ch):
                                first = (ci == 0 and pos == 0)
                                last = (ci == len(chunks) - 1 and
                                        pos == len(ch) - 1)
                                idx = 2 * (ci % 2)
                                t.matmul(
                                    ps_av[0:65, 2 * lhp, 0:128],
                                    v_g[:, kt,
                                        (2 * lhp) * 65:(2 * lhp) * 65 + 65],
                                    attnU[:, idx,
                                          pos * 128:(pos + 1) * 128],
                                    start=first, stop=last,
                                    skip_group_check=True)
                                ins = t.matmul(
                                    ps_av[0:65, 2 * lhp + 1, 0:128],
                                    v_g[:, kt, (2 * lhp + 1) * 65:
                                        (2 * lhp + 1) * 65 + 65],
                                    attnU[:, idx + 1,
                                          pos * 128:(pos + 1) * 128],
                                    start=first, stop=last,
                                    skip_group_check=True)
                            ins.then_inc(s_av.h, 1)

                        @blk.tensor
                        def _(t, g=g, j=j, chunks=chunks, bases=bases,
                              rz0=rz0):
                            for lhp in range(2):
                                mm0, al0, ex0, av0 = bases[lhp]
                                for ci, ch in enumerate(chunks):
                                    if ci >= 2:
                                        t.wait_ge(s_ex.h, ex0 + ci - 1)
                                    sl = ci % 2
                                    ins = None
                                    for pos, (kt, _m) in enumerate(ch):
                                        t.matmul(
                                            ps_sc[:, sl,
                                                  pos * 128:(pos + 1) * 128],
                                            kT_g[0:64, lhp,
                                                 kt * 128:(kt + 1) * 128],
                                            qT[0:64, 2 * g + lhp,
                                               j * 128:(j + 1) * 128],
                                            start=True, stop=True,
                                            tile_position=(0, 0))
                                        ins = t.matmul(
                                            ps_sc[:, 2 + sl,
                                                  pos * 128:(pos + 1) * 128],
                                            kT_g[64:128, lhp,
                                                 kt * 128:(kt + 1) * 128],
                                            qT[64:128, 2 * g + lhp,
                                               j * 128:(j + 1) * 128],
                                            start=True, stop=True,
                                            tile_position=(64, 0))
                                    ins.then_inc(s_mm.h, 1)
                                    if ci >= 1:
                                        t.wait_ge(s_ex.h, ex0 + ci)
                                        av_chunk(t, ci - 1, lhp)
                                t.wait_ge(s_ex.h, ex0 + len(chunks))
                                av_chunk(t, len(chunks) - 1, lhp)
                            t.wait_ge(c_rz.h, rz0 + 1)
                            ins = None
                            for i in range(4):
                                ins = t.matmul(ps_sc[0:64, i, 0:128],
                                               onesf_row[:, 0:64],
                                               rz[0:1, i, :],
                                               start=True, stop=True)
                            ins.then_inc(c_bz.h, 1)

                        @blk.vector
                        def _(v, g=g, j=j, chunks=chunks, bases=bases,
                              bz0=bz0):
                            for lhp in range(2):
                                mm0, al0, ex0, av0 = bases[lhp]
                                for ci, ch in enumerate(chunks):
                                    v.wait_ge(s_mm.h, mm0 + ci + 1)
                                    if ci >= 2:
                                        v.wait_ge(s_ex.h, ex0 + ci - 1)
                                    sl = ci % 2
                                    ins = None
                                    for pos, (kt, msk) in enumerate(ch):
                                        if msk is not None:
                                            for side in range(2):
                                                ins = v.tensor_tensor(
                                                    ps_sc[:, 2 * side + sl,
                                                          pos * 128:
                                                          (pos + 1) * 128],
                                                    ps_sc[:, 2 * side + sl,
                                                          pos * 128:
                                                          (pos + 1) * 128],
                                                    msk[:], op=ALU.add)
                                    if ins is None:
                                        ins = v.engine_nop()
                                    ins.then_inc(s_al.h, 1)
                            v.wait_ge(s_av.h, bases[1][3] + len(chunks))
                            for i in range(4):
                                v.tensor_copy(zrow[:, i, :],
                                              ps_av[64:65, i, 0:128])
                            ins = None
                            for i in range(4):
                                ins = v.reciprocal(rz[:, i, :], zrow[:, i, :])
                            ins.then_inc(c_rz.h, 1)
                            v.wait_ge(c_bz.h, bz0 + 1)
                            for i in range(4):
                                v.tensor_copy(bzs[:, i, :],
                                              ps_sc[0:64, i, 0:128])
                            for lhp in range(2):
                                for side in range(2):
                                    i = 2 * lhp + side
                                    dst = attn_oT[side * 64:side * 64 + 64,
                                                  2 * g + lhp,
                                                  j * 128:(j + 1) * 128]
                                    v.tensor_tensor(dst,
                                                    ps_av[0:64, i, 0:128],
                                                    bzs[:, i, :], op=ALU.mult)

                        @blk.scalar
                        def _(s2, g=g, chunks=chunks, bases=bases):
                            for lhp in range(2):
                                mm0, al0, ex0, av0 = bases[lhp]
                                hA, hB = 2 * (2 * g + lhp), 2 * (2 * g + lhp) + 1
                                for ci, ch in enumerate(chunks):
                                    s2.wait_ge(s_al.h, al0 + ci + 1)
                                    if ci >= 2:
                                        s2.wait_ge(s_av.h, av0 + ci - 1)
                                    sl = ci % 2
                                    w = len(ch) * 128
                                    s2.activation(attnU[:, 2 * (ci % 2), 0:w],
                                                  ps_sc[:, sl, 0:w], AF.Exp,
                                                  scale=0.125,
                                                  bias=cbias[:, hA:hA + 1])
                                    ins = s2.activation(
                                        attnU[:, 2 * (ci % 2) + 1, 0:w],
                                        ps_sc[:, 2 + sl, 0:w], AF.Exp,
                                        scale=0.125,
                                        bias=cbias[:, hB:hB + 1])
                                    ins.then_inc(s_ex.h, 1)
        if "aoT2" in dbg:
            em.store([(dbg["aoT2"], attn_oT[:])])

        with contextlib.ExitStack() as pc:
            sbp = lambda nm, shp, dt: pc.enter_context(
                nc.sbuf_tensor(nm, shp, dt))
            wnat = sbp("wonat", [128, 8, D], BF16)
            s2T = sbp("s2T", [128, 8, SH], F32)
            ps_wo = pc.enter_context(nc.psum_tensor("ps_wo1", [128, 4, 512],
                                                    F32))
            with nc.Block() as blk:
                @blk.gpsimd
                def _(gp):
                    for dc in range(8):
                        em.gld.inc(gp.dma_start(
                            out=wnat[:, dc, :],
                            in_=io["fwo"][dc * 128:(dc + 1) * 128, :]))
                    gp.wait_ge(em.gld.h, em.gld.v)
            for tb in range(4):
                for dg in range(2):
                    with nc.Block() as blk:
                        @blk.tensor
                        def _(t, tb=tb, dg=dg):
                            for doi in range(4):
                                do = dg * 4 + doi
                                for dc in range(8):
                                    t.matmul(ps_wo[:, doi, :],
                                             wnat[:, dc,
                                                  do * 128:(do + 1) * 128],
                                             attn_oT[:, dc,
                                                     tb * 512:(tb + 1) * 512],
                                             start=(dc == 0), stop=(dc == 7))
                    with nc.Block() as blk:
                        @blk.vector
                        def _(v, tb=tb, dg=dg):
                            for doi in range(4):
                                do = dg * 4 + doi
                                v.tensor_scalar(
                                    s2T[:, do, tb * 512:(tb + 1) * 512],
                                    ps_wo[:, doi, :],
                                    bcols[:, BOC, do:do + 1], None,
                                    op0=ALU.add)

            ps_tr = pc.enter_context(nc.psum_tensor("ps_tr1", [128, 8, 128],
                                                    F32))
            yin = sbp("yin", [128, 2, D], F32)
            y2t = sbp("y2t", [128, 2, D], F32)
            n2f = sbp("n2f", [128, D], F32)
            n2b = sbp("n2b", [128, D], BF16)
            n2lo = sbp("n2lo", [128, D], BF16)
            sq2 = sbp("sq2", [128, D], F32)
            stt_ = sbp("stt_", [128, 8], F32)
            y2s = sbp("y2s", [128, 8, 128], F32)
            for tt in range(16):
                with nc.Block() as blk:
                    @blk.sync
                    def _(s, tt=tt):
                        em.ld.inc(s.dma_start(
                            out=yin[:, tt % 2, :],
                            in_=scr["y_own_f32"][tt * 128:(tt + 1) * 128, :]))

                    @blk.tensor
                    def _(t, tt=tt):
                        for dc in range(8):
                            t.transpose(ps_tr[:, dc, :],
                                        s2T[:, dc, tt * 128:(tt + 1) * 128],
                                        identf_sb[:])
                with nc.Block() as blk:
                    @blk.vector
                    def _(v, tt=tt):
                        v.wait_ge(em.ld.h, em.ld.v)
                        for dc in range(8):
                            v.tensor_tensor(
                                y2t[:, tt % 2, dc * 128:(dc + 1) * 128],
                                ps_tr[:, dc, :],
                                yin[:, tt % 2, dc * 128:(dc + 1) * 128],
                                op=ALU.add)
                with nc.Block() as blk:
                    @blk.scalar
                    def _(s2, tt=tt):
                        s2.activation(sq2[:], y2t[:, tt % 2, :], AF.Square,
                                      accum_out=stt_[:, 0:1])

                    @blk.vector
                    def _(v, tt=tt):
                        v.reduce_sum(stt_[:, 1:2], y2t[:, tt % 2, :],
                                     axis=AX.X)
                with nc.Block() as blk:
                    @blk.vector
                    def _(v):
                        v.tensor_scalar_mul(stt_[:, 2:3], stt_[:, 1:2],
                                            1.0 / D)
                        v.tensor_scalar_mul(stt_[:, 3:4], stt_[:, 0:1],
                                            1.0 / D)
                        v.tensor_tensor(stt_[:, 4:5], stt_[:, 2:3],
                                        stt_[:, 2:3], op=ALU.mult)
                        v.tensor_tensor(stt_[:, 3:4], stt_[:, 3:4],
                                        stt_[:, 4:5], op=ALU.subtract)
                with nc.Block() as blk:
                    @blk.scalar
                    def _(s2):
                        s2.activation(stt_[:, 5:6], stt_[:, 3:4], AF.Sqrt,
                                      bias=epscol[:])
                with nc.Block() as blk:
                    @blk.vector
                    def _(v):
                        v.reciprocal(stt_[:, 6:7], stt_[:, 5:6])
                with nc.Block() as blk:
                    @blk.vector
                    def _(v, tt=tt):
                        v.tensor_scalar(n2f[:], y2t[:, tt % 2, :],
                                        stt_[:, 2:3], stt_[:, 6:7],
                                        op0=ALU.subtract, op1=ALU.mult)
                with nc.Block() as blk:
                    @blk.vector
                    def _(v):
                        v.tensor_copy(n2b[:], n2f[:])
                with nc.Block() as blk:
                    @blk.vector
                    def _(v):
                        v.tensor_tensor(n2lo[:], n2f[:], n2b[:],
                                        op=ALU.subtract)

                    @blk.tensor
                    def _(t, tt=tt):
                        for dc in range(8):
                            t.transpose(ps_tr[:, dc, :],
                                        y2t[:, tt % 2,
                                            dc * 128:(dc + 1) * 128],
                                        identf_sb[:])
                with nc.Block() as blk:
                    @blk.vector
                    def _(v):
                        for dc in range(8):
                            v.tensor_copy(y2s[:, dc, :], ps_tr[:, dc, :])
                with nc.Block() as blk:
                    @blk.gpsimd
                    def _(gp, tt=tt):
                        em.gso.inc(gp.dma_start(
                            out=scr["n2_own_bf"][tt * 128:(tt + 1) * 128, :],
                            in_=n2b[:]))
                        em.gso.inc(gp.dma_start(
                            out=scr["n2_own_lo"][tt * 128:(tt + 1) * 128, :],
                            in_=n2lo[:]))
                        gp.wait_ge(em.gso.h, em.gso.v)

                    @blk.sync
                    def _(s, tt=tt):
                        em.so.inc(s.dma_start(
                            out=scr["y2T_f32"][:, :, tt * 128:(tt + 1) * 128],
                            in_=y2s[:]))
                        s.wait_ge(em.so.h, em.so.v)
        if "n2" in dbg:
            em.store([(dbg["n2"], scr["n2_own_bf"])])
        if "y2T" in dbg:
            em.store([(dbg["y2T"], scr["y2T_f32"])])


def _l1_moe(em, io, scr, dbg, bcols, L2S, L2H, ones_row, onesf_row, identf_sb):
    nc = em.nc
    with contextlib.ExitStack() as scA:
        sbA = lambda nm, shp, dt: scA.enter_context(nc.sbuf_tensor(nm, shp, dt))
        n2g_h = sbA("n2g_h", [128, 8, SH], BF16)
        bgw_bc = sbA("bgw_bc", [128, E], F32)

        gate_scope = scA.enter_context(contextlib.ExitStack())
        n2g_lo = gate_scope.enter_context(
            nc.sbuf_tensor("n2g_lo", [128, 8, SH], BF16))
        with contextlib.ExitStack() as pc:
            sbp = lambda nm, shp, dt: pc.enter_context(
                nc.sbuf_tensor(nm, shp, dt))
            n2T = sbp("n2T", [128, 8, SH], BF16)
            n2loT = sbp("n2loT", [128, 8, SH], BF16)
            tf = sbp("tf", [128, 512], F32)
            tlo = sbp("tlo", [128, 512], BF16)
            em.load_wT(n2T, scr["n2_own_bf"], rows=SH)
            em.load_wT(n2loT, scr["n2_own_lo"], rows=SH)
            with nc.Block() as blk:
                @blk.vector
                def _(v):
                    for dc in range(8):
                        for tb in range(4):
                            v.tensor_scalar(
                                tf[:], n2T[:, dc, tb * 512:(tb + 1) * 512],
                                bcols[:, L2S, dc:dc + 1], None, op0=ALU.mult)
                            v.tensor_copy(
                                n2g_h[:, dc, tb * 512:(tb + 1) * 512], tf[:])
                            v.tensor_tensor(
                                tf[:], tf[:],
                                n2g_h[:, dc, tb * 512:(tb + 1) * 512],
                                op=ALU.subtract)
                            v.tensor_scalar(
                                tlo[:], n2loT[:, dc, tb * 512:(tb + 1) * 512],
                                bcols[:, L2S, dc:dc + 1], None, op0=ALU.mult)
                            v.tensor_tensor(
                                n2g_lo[:, dc, tb * 512:(tb + 1) * 512],
                                tf[:], tlo[:], op=ALU.add)

        with contextlib.ExitStack() as pc:
            sbp = lambda nm, shp, dt: pc.enter_context(
                nc.sbuf_tensor(nm, shp, dt))
            psb = lambda nm, shp: pc.enter_context(nc.psum_tensor(nm, shp, F32))
            gwf = sbp("gwf_sb", [128, 8, E], F32)
            gwh = sbp("gwh", [128, 8, 2 * E], BF16)
            b2bf = sbp("b2bf", [128, 8], F32)
            ps_lg = psb("ps_lg1", [128, 4, 512])
            ps_g = psb("ps_g1", [128, 512])
            lgt = sbp("lgt1", [128, 16, E], F32)
            g2_ = sbp("g2_1", [128, 16, 14], F32)
            eqs = sbp("eqs1", [128, 16, 2 * E], F32)
            gates = sbp("gates1", [128, 16, E], F32)
            gatesb = sbp("gatesb1", [128, 16, E], F32)
            gt_ps = psb("gt_ps1", [128, 2, 512])
            gt_sb = sbp("gt_sb1", [E, SH], F32)

            with nc.Block() as blk:
                @blk.sync
                def _(s):
                    for dc in range(8):
                        em.ld.inc(s.dma_start(
                            out=gwf[:, dc, :],
                            in_=io["gwf"][dc * 128:(dc + 1) * 128, :]))
                    s.wait_ge(em.ld.h, em.ld.v)
            with nc.Block() as blk:
                @blk.vector
                def _(v):
                    for dc in range(8):
                        v.tensor_copy(gwh[:, dc, 0:E], gwf[:, dc, :])
                    v.tensor_copy(b2bf[:], bcols[:, L2H, :])
            with nc.Block() as blk:
                @blk.vector
                def _(v):
                    for dc in range(8):
                        v.tensor_tensor(gwh[:, dc, E:2 * E], gwf[:, dc, :],
                                        gwh[:, dc, 0:E], op=ALU.subtract)
            with nc.Block() as blk:
                @blk.tensor
                def _(t):
                    for dc in range(8):
                        t.matmul(ps_g[0:1, 0:E], b2bf[:, dc:dc + 1],
                                 gwf[:, dc, :], start=(dc == 0), stop=(dc == 7))
            with nc.Block() as blk:
                @blk.vector
                def _(v):
                    v.tensor_copy(lgt[0:1, 0, :], ps_g[0:1, 0:E])
            with nc.Block() as blk:
                @blk.tensor
                def _(t):
                    t.matmul(ps_g[:, 0:E], onesf_row[:], lgt[0:1, 0, :],
                             start=True, stop=True)
            with nc.Block() as blk:
                @blk.vector
                def _(v):
                    v.tensor_copy(bgw_bc[:], ps_g[:, 0:E])

            for t0 in range(0, 16, 4):
                with nc.Block() as blk:
                    @blk.tensor
                    def _(t, t0=t0):
                        for i in range(4):
                            tt = t0 + i
                            first = True
                            for xb, ws in ((n2g_h, 0), (n2g_h, 1),
                                           (n2g_lo, 0), (n2g_lo, 1)):
                                for dc in range(8):
                                    t.matmul(ps_lg[:, i, 0:E],
                                             xb[:, dc, tt * 128:(tt + 1) * 128],
                                             gwh[:, dc, ws * E:(ws + 1) * E],
                                             start=first,
                                             stop=(ws == 1 and xb is n2g_lo and
                                                   dc == 7))
                                    first = False
                with nc.Block() as blk:
                    @blk.vector
                    def _(v, t0=t0):
                        for i in range(4):
                            v.tensor_tensor(lgt[:, t0 + i, :],
                                            ps_lg[:, i, 0:E], bgw_bc[:],
                                            op=ALU.add)

            _pend = []

            def vblock(fn):
                _pend.append(fn)

            def vflush():
                fns = list(_pend)
                del _pend[:]
                if not fns:
                    return
                with nc.Block() as blk:
                    @blk.vector
                    def _(v):
                        for fn in fns:
                            fn(v)

            T16 = range(16)
            vblock(lambda v: [v.reduce_max(g2_[:, t, 0:1], lgt[:, t, :],
                                           axis=AX.X) for t in T16])
            vblock(lambda v: [v.tensor_scalar(eqs[:, t, 0:E], lgt[:, t, :],
                                              g2_[:, t, 0:1], None,
                                              op0=ALU.is_equal) for t in T16])
            vblock(lambda v: [v.scalar_tensor_tensor(
                g2_[:, t, 5:13], eqs[:, t, 0:E], -1e9, lgt[:, t, :],
                op0=ALU.mult, op1=ALU.add) for t in T16])
            vblock(lambda v: [v.reduce_max(g2_[:, t, 1:2], g2_[:, t, 5:13],
                                           axis=AX.X) for t in T16])
            vblock(lambda v: [v.tensor_scalar(eqs[:, t, E:2 * E],
                                              g2_[:, t, 5:13], g2_[:, t, 1:2],
                                              None, op0=ALU.is_equal)
                              for t in T16])
            vblock(lambda v: [v.tensor_tensor(g2_[:, t, 2:3], g2_[:, t, 1:2],
                                              g2_[:, t, 0:1], op=ALU.subtract)
                              for t in T16])
            vflush()
            with nc.Block() as blk:
                @blk.scalar
                def _(s2):
                    for t in T16:
                        s2.activation(g2_[:, t, 3:4], g2_[:, t, 2:3], AF.Exp)
            vblock(lambda v: [v.tensor_scalar_add(g2_[:, t, 4:5],
                                                  g2_[:, t, 3:4], 1.0)
                              for t in T16])
            vblock(lambda v: [v.reciprocal(g2_[:, t, 13:14], g2_[:, t, 4:5])
                              for t in T16])
            vblock(lambda v: [v.tensor_tensor(g2_[:, t, 4:5], g2_[:, t, 3:4],
                                              g2_[:, t, 13:14], op=ALU.mult)
                              for t in T16])
            vblock(lambda v: [v.tensor_scalar(gates[:, t, :], eqs[:, t, 0:E],
                                              g2_[:, t, 13:14], None,
                                              op0=ALU.mult) for t in T16])
            vblock(lambda v: [v.scalar_tensor_tensor(
                gatesb[:, t, :], eqs[:, t, E:2 * E], g2_[:, t, 4:5],
                gates[:, t, :], op0=ALU.mult, op1=ALU.add) for t in T16])
            vflush()
            if "gates2" in dbg:
                em.store([(dbg["gates2"], gatesb[:])])
            g_tr = Sem(em.ctx, nc, "g1_tr")
            g_cp = Sem(em.ctx, nc, "g1_cp")
            g_tr.v += 16
            g_cp.v += 16
            with nc.Block() as blk:
                @blk.tensor
                def _(t):
                    for tt in range(16):
                        if tt >= 2:
                            t.wait_ge(g_cp.h, tt - 1)
                        ins = t.transpose(gt_ps[0:E, tt % 2, 0:128],
                                          gatesb[:, tt, :], identf_sb[:])
                        ins.then_inc(g_tr.h, 1)

                @blk.vector
                def _(v):
                    for tt in range(16):
                        v.wait_ge(g_tr.h, tt + 1)
                        ins = v.tensor_copy(gt_sb[:, tt * 128:(tt + 1) * 128],
                                            gt_ps[0:E, tt % 2, 0:128])
                        ins.then_inc(g_cp.h, 1)
            em.store([(scr["grow2"], gt_sb[:])])
        gate_scope.close()

        with contextlib.ExitStack() as pc:
            sbp = lambda nm, shp, dt: pc.enter_context(
                nc.sbuf_tensor(nm, shp, dt))
            w1e = sbp("w1e", [128, 8, HID1], BF16)
            w2e = sbp("w2e", [128, 8, HID1], BF16)
            w3e = sbp("w3e", [128, 16, 512], BF16)
            acc = sbp("macc", [128, 8, 1024], F32)
            hprod = sbp("hprod", [128, 16, 512], BF16)
            n2gg = sbp("n2gg", [128, 8, 512], BF16)
            groww = sbp("groww", [1, E, 1024], BF16)
            ones512 = sbp("ones512", [1, 512], BF16)
            b2bf2 = sbp("b2bf2", [128, 8], BF16)
            bwr = sbp("bwr", [1, 2, HID1], BF16)
            ps_b = pc.enter_context(nc.psum_tensor("mps_b", [128, 8, 512], F32))
            gout = sbp("gout", [128, 2, 1024], BF16)
            with nc.Block() as blk:
                @blk.vector
                def _(v):
                    v.tensor_copy(b2bf2[:], bcols[:, L2H, :])

                @blk.gpsimd
                def _(gp):
                    gp.memset(ones512[:], 1.0)
            t_bw = Sem(em.ctx, nc, "mt_bw")
            v_bw = Sem(em.ctx, nc, "mv_bw")
            t_bc = Sem(em.ctx, nc, "mt_bc")
            v_gg = Sem(em.ctx, nc, "mv_gg")
            t_br = Sem(em.ctx, nc, "mt_br")
            s_sl = Sem(em.ctx, nc, "ms_sl")
            v_pr = Sem(em.ctx, nc, "mv_pr")
            t_w3 = Sem(em.ctx, nc, "mt_w3")
            v_ac = Sem(em.ctx, nc, "mv_ac")
            for half in range(2):
                em.load([(acc[:], scr["y2T_f32"][:, :,
                                                 half * 1024:(half + 1) * 1024])])
                with nc.Block() as blk:
                    @blk.gpsimd
                    def _(gp, half=half):
                        for e in range(E):
                            em.gld.inc(gp.dma_start(
                                out=groww[0:1, e, :],
                                in_=scr["grow2"][e:e + 1,
                                                 half * 1024:(half + 1) * 1024]))
                        gp.wait_ge(em.gld.h, em.gld.v)
                base = dict(bw=t_bw.v, bc=t_bc.v, br=t_br.v, w3=t_w3.v,
                            sl=s_sl.v, pr=v_pr.v, ac=v_ac.v, vbw=v_bw.v,
                            gg=v_gg.v)
                ld_mk = {}
                with nc.Block() as blk:
                    @blk.sync
                    def _(s, base=base):
                        for e in range(E):
                            if e > 0:
                                s.wait_ge(t_br.h, base["br"] + 32 * e)
                            for dc in range(8):
                                em.ld.inc(s.dma_start(
                                    out=w1e[:, dc, :],
                                    in_=scr["w1_all"][e * D + dc * 128:
                                                      e * D + (dc + 1) * 128,
                                                      :]))
                            for dc in range(8):
                                em.ld.inc(s.dma_start(
                                    out=w2e[:, dc, :],
                                    in_=scr["w2_all"][e * D + dc * 128:
                                                      e * D + (dc + 1) * 128,
                                                      :]))
                            ld_mk[(e, "w12")] = em.ld.v
                            for rnd in range(4):
                                doh = rnd % 2
                                if e > 0 or rnd > 0:
                                    s.wait_ge(t_w3.h, base["w3"] + 4 * e + rnd)
                                for hc in range(16):
                                    em.ld.inc(s.dma_start(
                                        out=w3e[:, hc, :],
                                        in_=scr["w3_all"][
                                            e * HID1 + hc * 128:
                                            e * HID1 + (hc + 1) * 128,
                                            doh * 512:(doh + 1) * 512]))
                                ld_mk[(e, "w3", rnd)] = em.ld.v

                    @blk.tensor
                    def _(t, half=half, base=base):
                        for e in range(E):
                            vb0 = base["vbw"] + e
                            gg0 = base["gg"] + 2 * e
                            pr0 = base["pr"] + 32 * e
                            ac0 = base["ac"] + 4 * e
                            if e > 0:
                                t.wait_ge(v_ac.h, ac0)
                            t.wait_ge(em.ld.h, ld_mk[(e, "w12")])
                            for br, wb in ((0, w1e), (1, w2e)):
                                for hh in range(4):
                                    for dc in range(8):
                                        ins = t.matmul(
                                            ps_b[0:1, br * 4 + hh, :],
                                            b2bf2[:, dc:dc + 1],
                                            wb[:, dc, hh * 512:(hh + 1) * 512],
                                            start=(dc == 0), stop=(dc == 7))
                            ins.then_inc(t_bw.h, 1)
                            t.wait_ge(v_bw.h, vb0 + 1)
                            for tb2 in range(2):
                                if tb2 == 1:
                                    t.wait_ge(v_ac.h, ac0 + 2)
                                ins = t.matmul(ps_b[:, 6, :], ones_row[:],
                                               groww[0:1, e,
                                                     tb2 * 512:(tb2 + 1) * 512],
                                               start=True, stop=True)
                                ins.then_inc(t_bc.h, 1)
                                t.wait_ge(v_gg.h, gg0 + tb2 + 1)
                                for ht in range(16):
                                    if ht >= 2:
                                        t.wait_ge(v_pr.h,
                                                  pr0 + tb2 * 16 + ht - 1)
                                    pb = ht % 2
                                    for dc in range(8):
                                        t.matmul(
                                            ps_b[:, pb, :],
                                            w1e[:, dc, ht * 128:(ht + 1) * 128],
                                            n2g_h[:, dc,
                                                  half * 1024 + tb2 * 512:
                                                  half * 1024 +
                                                  (tb2 + 1) * 512],
                                            start=(dc == 0), stop=False)
                                    t.matmul(ps_b[:, pb, :],
                                             bwr[0:1, 0,
                                                 ht * 128:(ht + 1) * 128],
                                             ones512[:],
                                             start=False, stop=True)
                                    for dc in range(8):
                                        t.matmul(
                                            ps_b[:, 2 + pb, :],
                                            w2e[:, dc, ht * 128:(ht + 1) * 128],
                                            n2gg[:, dc, :],
                                            start=(dc == 0), stop=False)
                                    ins = t.matmul(
                                        ps_b[:, 2 + pb, :],
                                        bwr[0:1, 1, ht * 128:(ht + 1) * 128],
                                        groww[0:1, e,
                                              tb2 * 512:(tb2 + 1) * 512],
                                        start=False, stop=True)
                                    ins.then_inc(t_br.h, 1)
                                t.wait_ge(v_pr.h, pr0 + (tb2 + 1) * 16)
                                for doh in range(2):
                                    rnd = tb2 * 2 + doh
                                    t.wait_ge(em.ld.h, ld_mk[(e, "w3", rnd)])
                                    if rnd > 0:
                                        t.wait_ge(v_ac.h, ac0 + rnd)
                                    for doi in range(4):
                                        for hc in range(16):
                                            ins = t.matmul(
                                                ps_b[:, 4 + doi, :],
                                                w3e[:, hc,
                                                    doi * 128:(doi + 1) * 128],
                                                hprod[:, hc, :],
                                                start=(hc == 0),
                                                stop=(hc == 15))
                                    ins.then_inc(t_w3.h, 1)

                    @blk.scalar
                    def _(s2, base=base):
                        for e in range(E):
                            br0 = base["br"] + 32 * e
                            w30 = base["w3"] + 4 * e
                            sl_base = base["sl"] + 32 * e
                            for tb2 in range(2):
                                for ht in range(16):
                                    s2.wait_ge(t_br.h, br0 + tb2 * 16 + ht + 1)
                                    if tb2 == 1 and ht == 0:
                                        s2.wait_ge(t_w3.h, w30 + 2)
                                    pb = ht % 2
                                    ins = s2.activation(hprod[:, ht, :],
                                                        ps_b[:, pb, :],
                                                        AF.Silu)
                                    ins.then_inc(s_sl.h, 1)

                    @blk.vector
                    def _(v, half=half, base=base):
                        for e in range(E):
                            bw0 = base["bw"] + e
                            bc0 = base["bc"] + 2 * e
                            sl0 = base["sl"] + 32 * e
                            w30 = base["w3"] + 4 * e
                            v.wait_ge(t_bw.h, bw0 + 1)
                            for br in range(2):
                                for hh in range(4):
                                    ins = v.tensor_copy(
                                        bwr[0:1, br, hh * 512:(hh + 1) * 512],
                                        ps_b[0:1, br * 4 + hh, :])
                            ins.then_inc(v_bw.h, 1)
                            for tb2 in range(2):
                                v.wait_ge(t_bc.h, bc0 + tb2 + 1)
                                for dc in range(8):
                                    ins = v.tensor_tensor(
                                        n2gg[:, dc, :],
                                        n2g_h[:, dc,
                                              half * 1024 + tb2 * 512:
                                              half * 1024 + (tb2 + 1) * 512],
                                        ps_b[:, 6, :], op=ALU.mult)
                                ins.then_inc(v_gg.h, 1)
                                for ht in range(16):
                                    v.wait_ge(s_sl.h, sl0 + tb2 * 16 + ht + 1)
                                    pb = ht % 2
                                    ins = v.tensor_tensor(hprod[:, ht, :],
                                                          hprod[:, ht, :],
                                                          ps_b[:, 2 + pb, :],
                                                          op=ALU.mult)
                                    ins.then_inc(v_pr.h, 1)
                                for doh in range(2):
                                    rnd = tb2 * 2 + doh
                                    v.wait_ge(t_w3.h, w30 + rnd + 1)
                                    for doi in range(4):
                                        do = doh * 4 + doi
                                        ins = v.tensor_tensor(
                                            acc[:, do,
                                                tb2 * 512:(tb2 + 1) * 512],
                                            acc[:, do,
                                                tb2 * 512:(tb2 + 1) * 512],
                                            ps_b[:, 4 + doi, :], op=ALU.add)
                                    ins.then_inc(v_ac.h, 1)
                t_bw.v += E
                t_bc.v += 2 * E
                t_br.v += 32 * E
                t_w3.v += 4 * E
                s_sl.v += 32 * E
                v_bw.v += E
                v_gg.v += 2 * E
                v_pr.v += 32 * E
                v_ac.v += 4 * E
                for d0 in range(0, 8, 2):
                    with nc.Block() as blk:
                        @blk.scalar
                        def _(s2, d0=d0):
                            for i in range(2):
                                s2.activation(gout[:, i, :],
                                              acc[:, d0 + i, :], AF.Silu)
                    em.store([(scr["g2_dram"][(d0 + i) * 128:
                                              (d0 + i + 1) * 128,
                                              half * 1024:(half + 1) * 1024],
                               gout[:, i, :]) for i in range(2)])
    if "g2" in dbg:
        em.store([(dbg["g2"], scr["g2_dram"])])


def _l1_final(em, io, scr, dbg, identf_sb):
    nc = em.nc
    with contextlib.ExitStack() as pc:
        sbp = lambda nm, shp, dt: pc.enter_context(nc.sbuf_tensor(nm, shp, dt))
        s_T = sbp("fs_T", [128, 8, SH], F32)
        gT = sbp("fgT", [128, 8, SH], BF16)
        ut = sbp("fut", [128, 8, 128], F32)
        xin2 = sbp("fxin", [128, 2, D], F32)
        xoutf = sbp("fxout", [128, 2, D], F32)
        ps_tr = pc.enter_context(nc.psum_tensor("fps_tr", [128, 8, 128], F32))
        em.load([(s_T[:], scr["s_dram"]),
                 (gT[:], scr["g2_dram"].rearrange("(a p) n -> p a n", p=128))])
        for tt in range(16):
            with nc.Block() as blk:
                @blk.sync
                def _(s, tt=tt):
                    em.ld.inc(s.dma_start(
                        out=xin2[:, tt % 2, :],
                        in_=(scr["x1_dram"] if "x1_dram" in scr else
                             io["x1_own"])[tt * 128:(tt + 1) * 128, :]))

                @blk.vector
                def _(v, tt=tt):
                    for dc in range(8):
                        v.tensor_tensor(ut[:, dc, :],
                                        s_T[:, dc, tt * 128:(tt + 1) * 128],
                                        gT[:, dc, tt * 128:(tt + 1) * 128],
                                        op=ALU.mult)
            with nc.Block() as blk:
                @blk.tensor
                def _(t, tt=tt):
                    for dc in range(8):
                        t.transpose(ps_tr[:, dc, :], ut[:, dc, :],
                                    identf_sb[:])
            with nc.Block() as blk:
                @blk.vector
                def _(v, tt=tt):
                    v.wait_ge(em.ld.h, em.ld.v)
                    for dc in range(8):
                        v.tensor_tensor(
                            xoutf[:, tt % 2, dc * 128:(dc + 1) * 128],
                            ps_tr[:, dc, :],
                            xin2[:, tt % 2, dc * 128:(dc + 1) * 128],
                            op=ALU.add)
            em.store([(io["xout_own"][tt * 128:(tt + 1) * 128, :],
                       xoutf[:, tt % 2, :])])


def stage_l1(inputs, x1, core):
    b, p = core // 2, core % 2
    own_tiles = np.arange(p, 32, 2)
    eg_tiles = _eg_tiles()
    if x1 is None:
        xt = np.zeros((32, 1, D), np.float32)
        own_tiles0, eg_tiles0 = own_tiles, eg_tiles
        x1o = np.zeros((1, 1), np.float32)
    else:
        xt = np.ascontiguousarray(x1[b]).reshape(32, 128, D)

    def col8(v):
        return np.ascontiguousarray(
            np.asarray(v, np.float32).reshape(8, 128).T)

    tri = np.where(np.arange(128)[:, None] > np.arange(128)[None, :],
                   np.float32(-240000.0), np.float32(0.0))
    zero = np.zeros((128, 128), np.float32)
    negf = np.full((128, 128), -240000.0, np.float32)
    mask_a = tri if p == 0 else zero
    mask_b = negf if p == 0 else tri
    flag = np.zeros((128, 2), np.float32)
    flag[:, 0] = 1.0 if p == 0 else 0.0
    flag[:, 1] = 0.0 if p == 0 else 1.0
    if x1 is None:
        x1_own = x1_eg = np.zeros((1, 1), np.float32)
    else:
        x1_own = np.ascontiguousarray(xt[own_tiles].reshape(SH, D))
        x1_eg = np.ascontiguousarray(xt[eg_tiles].reshape(S, D))
    return {
        "x1_own": x1_own,
        "x1_eg": x1_eg,
        "wgA": np.asarray(inputs["L1_attn_Wg"], np.float32),
        "wvA": np.asarray(inputs["L1_attn_Wv"], np.float32),
        "wdA": np.asarray(inputs["L1_attn_Wd"], np.float32),
        "bgc": col8(inputs["L1_attn_bg"]),
        "bvc": col8(inputs["L1_attn_bv"]),
        "bdc": col8(inputs["L1_attn_bd"]),
        "l1s": col8(inputs["L1_ffn_ln1_scale"]),
        "l1h": col8(inputs["L1_ffn_ln1_shift"]),
        "l2s": col8(inputs["L1_ffn_ln2_scale"]),
        "l2h": col8(inputs["L1_ffn_ln2_shift"]),
        "fwq": np.asarray(inputs["L1_ffn_wq"], np.float32),
        "fwk": np.asarray(inputs["L1_ffn_wk"], np.float32),
        "fwv": np.asarray(inputs["L1_ffn_wv"], np.float32),
        "fwo": np.asarray(inputs["L1_ffn_wo"], np.float32),
        "boc": col8(inputs["L1_ffn_bo"]),
        "gwf": np.asarray(inputs["L1_ffn_gate_w"], np.float32),
        "w1_own": np.asarray(inputs["L1_ffn_w1"][core], np.float32),
        "w2_own": np.asarray(inputs["L1_ffn_w2"][core], np.float32),
        "w3_own": np.asarray(inputs["L1_ffn_w3"][core], np.float32),
        "mask_a": mask_a, "mask_b": mask_b, "flag": flag,
        "identf": np.eye(128, dtype=np.float32),
    }


def build_fused(debug=()):
    nc = bass.Bass("TRN2", target_bir_lowering=False)
    io, scr = {}, {}

    def ein(nm, shape, dt=F32):
        io[nm] = nc.dram_tensor(nm, shape, dt, kind="ExternalInput").ap()

    def eout(nm, shape, dt=F32):
        io[nm] = nc.dram_tensor(nm, shape, dt, kind="ExternalOutput").ap()

    def dscr(nm, shape, dt=BF16):
        scr[nm] = nc.dram_tensor(nm, shape, dt)

    ein("x_own", [SH, D]); ein("x_eg", [S, D])
    ein("wq", [D, D]); ein("wk", [D, D]); ein("wv", [D, D]); ein("wo", [D, D])
    ein("wgate", [E, D])
    ein("win_own", [HID0, D]); ein("wout_own", [D, HID0])
    ein("cosq", [128, SH], FP16); ein("sinq", [128, SH], FP16)
    ein("cosk", [128, S], FP16); ein("sink", [128, S], FP16)
    ein("posq", [1, SH]); ein("posk", [128, 32])
    ein("identf", [128, 128])
    ein("wgA", [D, D]); ein("wvA", [D, D]); ein("wdA", [D, D])
    ein("bgc", [128, 8]); ein("bvc", [128, 8]); ein("bdc", [128, 8])
    ein("l1s", [128, 8]); ein("l1h", [128, 8])
    ein("l2s", [128, 8]); ein("l2h", [128, 8])
    ein("fwq", [D, D]); ein("fwk", [D, D]); ein("fwv", [D, D]); ein("fwo", [D, D])
    ein("boc", [128, 8])
    ein("gwf", [D, E])
    ein("w1_own", [D, HID1]); ein("w2_own", [D, HID1]); ein("w3_own", [HID1, D])
    ein("mask_a", [128, 128]); ein("mask_b", [128, 128])
    ein("flag", [128, 2])
    eout("xout_own", [SH, D])

    dscr("xn_eg_bf", [S, D]); dscr("xn_own_bf", [SH, D])
    dscr("xn_own_lo", [SH, D])
    dscr("wq_bf", [D, D]); dscr("wk_bf", [D, D]); dscr("wv_bf", [D, D])
    dscr("wo_bf", [D, D]); dscr("wout_bf", [D, HID0])
    dscr("kT_dram", [128, 8, S]); dscr("vS_dram", [128, 32, H * 65])
    dscr("prod_own", [D, D]); dscr("prod_all", [E * D, D])
    dscr("g_dram", [D, SH]); dscr("abs_dram", [4, 32, 128, 512], FP16)
    dscr("grow_dram", [E, SH], F32)
    dscr("x1_dram", [SH, D], F32)
    dscr("y_own_bf", [SH, D]); dscr("y_eg_bf", [S, D])
    dscr("y_own_f32", [SH, D], F32)
    dscr("n1_own_bf", [SH, D]); dscr("n1_eg_bf", [S, D])
    dscr("wgA_bf", [D, D]); dscr("wvA_bf", [D, D]); dscr("wdA_bf", [D, D])
    dscr("fwq_bf", [D, D]); dscr("fwk_bf", [D, D]); dscr("fwv_bf", [D, D])
    dscr("fwo_bf", [D, D])
    dscr("w1_bf", [D, HID1]); dscr("w2_bf", [D, HID1]); dscr("w3_bf", [HID1, D])
    dscr("w1_all", [E * D, HID1]); dscr("w2_all", [E * D, HID1])
    dscr("w3_all", [E * HID1, D])
    dscr("A_dram", [128, 8, S], F32); dscr("B_dram", [128, 8, S], F32)
    dscr("s_dram", [128, 8, SH], F32)
    dscr("y2T_f32", [128, 8, SH], F32)
    dscr("n2_own_bf", [SH, D]); dscr("n2_own_lo", [SH, D])
    dscr("grow2", [E, SH], F32)
    dscr("g2_dram", [D, SH])

    dbg = {}
    scr = {k: v.ap() for k, v in scr.items()}
    with contextlib.ExitStack() as ctx:
        em = Emitter(nc, ctx)
        wcbuf = ctx.enter_context(
            nc.sbuf_tensor("l1wcast", [128, 1, 1024], BF16))
        jobs = [("wq", io["wq"], scr["wq_bf"], D, D),
                ("wk", io["wk"], scr["wk_bf"], D, D),
                ("wv", io["wv"], scr["wv_bf"], D, D),
                ("wout_own", io["wout_own"], scr["wout_bf"], D, HID0),
                ("wo", io["wo"], scr["wo_bf"], D, D),
                ("wgA", io["wgA"], scr["wgA_bf"], D, D),
                ("wvA", io["wvA"], scr["wvA_bf"], D, D),
                ("wdA", io["wdA"], scr["wdA_bf"], D, D),
                ("w1", io["w1_own"], scr["w1_bf"], D, HID1),
                ("w2", io["w2_own"], scr["w2_bf"], D, HID1),
                ("w3", io["w3_own"], scr["w3_bf"], HID1, D)]
        feeder = CastFeeder(em, jobs, wcbuf, depth=1)
        _emit_l0(em, io, scr, dbg, x1_dst=scr["x1_dram"], feeder=feeder)
        _emit_l1(em, io, scr, dbg, fused=True)
    return nc


def stage_fused(inputs, core):
    d = stage_l0(inputs, core)
    d1 = stage_l1(inputs, None, core)
    d1.pop("x1_own"); d1.pop("x1_eg")
    d1.pop("identf")
    d.update(d1)
    return d


def kernel(**inputs):
    nc = build_fused()
    maps = [stage_fused(inputs, c) for c in range(8)]
    res = run_bass_kernel_spmd(nc, maps, core_ids=list(range(8)))
    return unstage_x1(res.results, key="xout_own")
